# version 14
# baseline (speedup 1.0000x reference)
import numpy as np
import contextlib
import concourse.bass as bass
import concourse.mybir as mybir
from concourse.bass_utils import run_bass_kernel_spmd

F32 = mybir.dt.float32
BF16 = mybir.dt.bfloat16
AF = mybir.ActivationFunctionType
ALU = mybir.AluOpType
AX = mybir.AxisListType

S, E, T, HD, NH, NKV, II = 4096, 2048, 512, 128, 16, 4, 8192
SCALE = float(1.0 / np.sqrt(128.0))
EPS = 1e-5
ENGINES = ("PE", "DVE", "SE", "SP", "POOL")


def build_nc(n_tiles=8):
    nc = bass.Bass("TRN2", target_bir_lowering=False, debug=False)

    x_h = nc.dram_tensor("x", [S, E], F32, kind="ExternalInput").ap()
    wg_h = nc.dram_tensor("wg_t", [16, 4, 128, 4, 512], F32, kind="ExternalInput").ap()
    wu_h = nc.dram_tensor("wu_t", [16, 4, 128, 4, 512], F32, kind="ExternalInput").ap()
    wd_h = nc.dram_tensor("wd_t", [II, E], F32, kind="ExternalInput").ap()
    wq_h = nc.dram_tensor("wq_t", [16, 128, 16, 128], F32, kind="ExternalInput").ap()
    wkv_h = nc.dram_tensor("wkv_t", [E, 1024], F32, kind="ExternalInput").ap()
    wo_h = nc.dram_tensor("wo_t", [E, E], F32, kind="ExternalInput").ap()
    bq_h = nc.dram_tensor("bq", [16, 128], F32, kind="ExternalInput").ap()
    bkv_h = nc.dram_tensor("bkv", [8, 128], F32, kind="ExternalInput").ap()
    bkvv_h = nc.dram_tensor("bkvv", [1, 512], F32, kind="ExternalInput").ap()
    bo_h = nc.dram_tensor("bo_r", [1, E], F32, kind="ExternalInput").ap()
    cos_h = nc.dram_tensor("cosfm", [128, S], F32, kind="ExternalInput").ap()
    sin_h = nc.dram_tensor("sinfm", [128, S], F32, kind="ExternalInput").ap()
    strip_h = nc.dram_tensor("strip", [128, 1408], BF16, kind="ExternalInput").ap()
    mb1_h = nc.dram_tensor("maskb1", [128, 512], BF16, kind="ExternalInput").ap()
    mmeta_h = nc.dram_tensor("maskmeta", [128, 512], BF16, kind="ExternalInput").ap()
    iden_h = nc.dram_tensor("idenf", [128, 128], F32, kind="ExternalInput").ap()
    prope_h = nc.dram_tensor("prope", [128, 128], F32, kind="ExternalInput").ap()
    onec_h = nc.dram_tensor("onec_bf", [128, 1], BF16, kind="ExternalInput").ap()
    oner_h = nc.dram_tensor("oner_f", [1, 128], F32, kind="ExternalInput").ap()
    out_h = nc.dram_tensor("out", [S, E], F32, kind="ExternalOutput").ap()

    wgb = nc.dram_tensor("wg_bf", [16, 4, 128, 4, 512], BF16, kind="Internal").ap()
    wub = nc.dram_tensor("wu_bf", [16, 4, 128, 4, 512], BF16, kind="Internal").ap()
    wdb = nc.dram_tensor("wd_bf", [II, E], BF16, kind="Internal").ap()
    wqb = nc.dram_tensor("wq_bf", [16, 128, 16, 128], BF16, kind="Internal").ap()
    wkvb = nc.dram_tensor("wkv_bf", [E, 1024], BF16, kind="Internal").ap()
    wob = nc.dram_tensor("wo_bf", [E, E], BF16, kind="Internal").ap()

    A = nc.alloc_sbuf_tensor
    x_res = A("x_res", [128, 4 * 2048], F32)
    xn_fm = A("xn_fm", [128, 16 * 512], BF16)
    k_ctx = A("k_ctx", [128, 4 * 1152], BF16)
    v_ctx = A("v_ctx", [128, 9 * 512], BF16)
    cos_t = A("cos_t", [128, 512], F32)
    sin_t = A("sin_t", [128, 512], F32)
    bo_bc = A("bo_bc", [128, 2048], F32)
    bkvv_bc = A("bkvv_bc", [128, 512], F32)
    stripb = A("stripb", [128, 1408], BF16)
    mb1b = A("mb1b", [128, 512], BF16)
    mmetab = A("mmetab", [128, 512], BF16)
    idenb = A("idenb", [128, 128], F32)
    propeb = A("propeb", [128, 128], F32)
    onecb = A("onecb", [128, 1], BF16)
    onerb = A("onerb", [1, 128], F32)
    bq_s = A("bq_s", [128, 16], F32)
    bkvk_s = A("bkvk_s", [128, 8], F32)
    bqraw = A("bqraw", [16, 128], F32)
    bkraw = A("bkraw", [8, 128], F32)
    stats = A("stats", [128, 36], F32)
    srow = A("srow", [1, 16], F32)
    arow = A("arow", [1, 1024], F32)
    nmrow = A("nmrow", [1, 2], F32)
    nm128 = A("nm128", [128, 2], F32)
    linv = A("linv", [1, 512], F32)
    h_f = A("h_f", [128, 2048], F32)
    a_f = A("a_f", [128, 2048], F32)
    tmp4 = A("tmp4", [128, 2048], F32)
    tsel = A("tsel", [128, 512], F32)
    mskf = A("mskf", [128, 3 * 512], F32)
    wd_ring = A("wd_ring", [128, 4 * 2048], BF16)
    arena = A("arena", [128, 16384], F32)
    arena_bf = arena.bitcast(BF16)
    hs = arena_bf[:, 0:16384]
    wg_ring = arena_bf[:, 16384:24576]
    wu_ring = arena_bf[:, 24576:32768]
    q_fm = arena_bf[:, 0:8192]
    p_sb = arena_bf[:, 8192:12800]
    qb_f = arena[:, 6400:7424]
    rt1 = arena[:, 7424:7936]
    rt2 = arena[:, 7936:8448]
    recip_bc = arena[:, 8448:8960]
    k2b = arena_bf[:, 17920:19072]
    q2b = arena_bf[:, 19072:19584]
    wkv_ring = arena_bf[:, 19584:23680]
    wq_ring = arena_bf[:, 23680:27776]
    wo_ring = arena_bf[:, 27776:31872]

    ps_a = nc.alloc_psum_tensor("ps_a", [128, 2048], F32)
    ps_b = nc.alloc_psum_tensor("ps_b", [128, 2048], F32)

    plan = {e: [] for e in ENGINES}

    class Ctr:
        __slots__ = ("name", "step", "n", "h")

        def __init__(self, name, step):
            self.name, self.step, self.n, self.h = name, step, 0, None

        def inc(self, k=1):
            self.n += k * self.step
            return (self, k * self.step)

    c_ldx = Ctr("ldx", 16)
    c_outs = [Ctr(f"ldo{i}", 16) for i in range(4)]
    c_xins = [Ctr(f"ldi{i}", 16) for i in range(4)]
    c_pe = Ctr("pe", 1)
    c_dv = Ctr("dv", 1)
    c_se = Ctr("se", 1)
    CTRS = [c_ldx, c_pe, c_dv, c_se] + c_outs + c_xins

    class Ring:

        def __init__(self, name, nslots):
            self.ctrs = [Ctr(f"{name}{i}", 16) for i in range(nslots)]
            self.free_at = [0] * nslots
            self.n = nslots
            CTRS.extend(self.ctrs)

        def load(self, eng, slot, fn, extra_waits=()):
            wts = list(extra_waits)
            if self.free_at[slot]:
                wts.append(W(c_pe, self.free_at[slot]))
            op(eng, fn, waits=wts, incs=[self.ctrs[slot].inc()])
            return W(self.ctrs[slot])

        def bump_all(self, val):
            for i in range(self.n):
                self.free_at[i] = max(self.free_at[i], val)

    r_wg = Ring("rwg", 4)
    r_wu = Ring("rwu", 4)
    r_wd = Ring("rwd", 4)
    r_wkv = Ring("rkv", 4)
    r_wq = Ring("rwq", 2)
    r_wo = Ring("rwo", 2)

    pending_edge = {}

    def op(eng, fn, waits=(), incs=()):
        pe_ = pending_edge.pop(eng, None)
        w = tuple(waits) + ((pe_,) if pe_ else ())
        plan[eng].append((w, fn, tuple(incs)))

    def W(ctr, val=None):
        return (ctr, ctr.n if val is None else val)

    def dma(dst, src):
        return lambda g, d=dst, s=src: g.dma_start(d, s)

    def edge(eng):
        c = {"DVE": c_dv, "SE": c_se, "PE": c_pe}[eng]
        waits, fn, incs = plan[eng][-1]
        for (ctr, amt) in incs:
            if ctr is c:
                pending_edge[eng] = (c, c.n)
                return
        assert not incs, "op already has a different-counter inc"
        plan[eng][-1] = (waits, fn, incs + (c.inc(),))
        pending_edge[eng] = (c, c.n)

    st_ = {
        "last_a": W(c_dv, 0), "last_b": W(c_dv, 0),
        "xres_wr": [W(c_dv, 0)] * 4,
        "rope_done": W(c_dv, 0),
        "cos_ready": 0,
        "out_done": [0, 0, 0, 0],
        "xn_ready": W(c_dv, 0),
        "tr_pe": [0, 0],
    }
    wg_serial = [0]
    wd_serial = [0]

    for dst, src in ((stripb[:, :], strip_h[:, :]), (mb1b[:, :], mb1_h[:, :]),
                     (mmetab[:, :], mmeta_h[:, :]), (idenb[:, :], iden_h[:, :]),
                     (propeb[:, :], prope_h[:, :]), (onecb[:, :], onec_h[:, :]),
                     (onerb[:, :], oner_h[:, :]), (bqraw[:, :], bq_h[:, :]),
                     (bkraw[:, :], bkv_h[:, :])):
        op("POOL", dma(dst, src), incs=[c_ldx.inc()])
    consts_ld = c_ldx.n
    op("POOL", dma(a_f[0:1, 0:2048], bo_h[:, :]), incs=[c_ldx.inc()])
    op("POOL", dma(tmp4[0:1, 0:512], bkvv_h[:, :]), incs=[c_ldx.inc()])
    bcast_ld = c_ldx.n

    op("PE", lambda t: t.transpose(ps_a[:, 0:16], bqraw[:, :], idenb[0:16, 0:16]),
       waits=[W(c_ldx, bcast_ld)], incs=[c_pe.inc()])
    op("PE", lambda t: t.transpose(ps_a[:, 512:520], bkraw[:, :], idenb[0:8, 0:8]),
       incs=[c_pe.inc()])
    pe_bias = c_pe.n
    op("SE", lambda s: s.activation(bq_s[:, :], ps_a[:, 0:16], AF.Identity, scale=SCALE),
       waits=[W(c_pe, pe_bias)])
    op("SE", lambda s: s.activation(bkvk_s[:, :], ps_a[:, 512:520], AF.Identity),
       incs=[c_se.inc()])
    se_bias = c_se.n
    for j in range(4):
        op("PE", lambda t, j=j: t.matmul(ps_b[:, j * 512:(j + 1) * 512], onerb[:, :],
                                         a_f[0:1, j * 512:(j + 1) * 512],
                                         start=True, stop=True),
           waits=([W(c_ldx, bcast_ld)] if j == 0 else []))
    op("PE", lambda t: t.matmul(ps_a[:, 0:512], onerb[:, :], tmp4[0:1, 0:512],
                                start=True, stop=True),
       waits=[W(c_se, se_bias)], incs=[c_pe.inc()])
    pe_bcast = c_pe.n
    op("DVE", lambda v: v.tensor_copy(bo_bc[:, :], ps_b[:, :]),
       waits=[W(c_pe, pe_bcast)])
    op("DVE", lambda v: v.tensor_copy(bkvv_bc[:, :], ps_a[:, 0:512]))
    op("DVE", lambda v: v.memset(stats[:, :], 0.0))
    op("DVE", lambda v: v.memset(k_ctx[:, :], 0.0))
    op("DVE", lambda v: v.memset(v_ctx[:, :], 0.0), incs=[c_dv.inc()])
    st_["last_a"] = W(c_dv)
    st_["last_b"] = W(c_dv)

    cast_after = {}
    cast_sems = []

    def cast_dma(key, fn):
        c = Ctr(f"cs{len(cast_sems)}", 16)
        cast_sems.append(c)
        CTRS.append(c)
        op("POOL", fn, incs=[c.inc()])
        cast_after[key] = W(c, 16)

    def cast_wg(pc):
        cast_dma(("wg2", pc), lambda g, pc=pc: g.dma_start(
            wgb[pc * 2:(pc + 1) * 2], wg_h[pc * 2:(pc + 1) * 2]))
        cast_dma(("wu2", pc), lambda g, pc=pc: g.dma_start(
            wub[pc * 2:(pc + 1) * 2], wu_h[pc * 2:(pc + 1) * 2]))
        cast_after[("wg", pc)] = cast_after[("wg2", pc)]
        cast_after[("wu", pc)] = cast_after[("wu2", pc)]

    def cast_wdq(pc):
        cast_dma(("wdq", pc), lambda g, pc=pc: g.dma_start(
            wdb[pc * 2048:(pc + 1) * 2048, :], wd_h[pc * 2048:(pc + 1) * 2048, :]))

    cast_wg(0); cast_wg(1); cast_wg(2); cast_wg(3)
    cast_wdq(0); cast_wdq(1)
    cast_wg(4); cast_wg(5); cast_wg(6); cast_wg(7)
    cast_wdq(2); cast_wdq(3)
    cast_dma("wkv", lambda g: g.dma_start(wkvb[:, :], wkv_h[:, :]))
    cast_dma("wq", lambda g: g.dma_start(wqb[:, :, :, :], wq_h[:, :, :, :]))
    cast_dma("wo", lambda g: g.dma_start(wob[:, :], wo_h[:, :]))

    ln_round = [0]
    ln_evac = [0, 0]

    def ln_emit():
        ln_evac[0] = ln_evac[1] = 0
        ln_round[0] = 1
        tr_pe = st_["tr_pe"]
        for sub in range(4):
            xa = x_res[:, sub * 2048:(sub + 1) * 2048]
            wx = st_["xres_wr"][sub]
            op("DVE", lambda v, xa=xa, sub=sub: v.tensor_reduce(
                stats[:, sub:sub + 1], xa, AX.X, ALU.add),
               waits=[wx, W(c_dv)])
            op("SE", lambda s, xa=xa: s.activation(h_f[:, :], xa, AF.Square),
               waits=[wx, W(c_dv)], incs=[c_se.inc()])
            op("DVE", lambda v, sub=sub: v.tensor_reduce(
                stats[:, 4 + sub:5 + sub], h_f[:, :], AX.X, ALU.add),
               waits=[W(c_se)], incs=[c_dv.inc()])
            edge("DVE")
            op("DVE", lambda v, sub=sub: v.tensor_scalar_mul(
                stats[:, 8 + sub:9 + sub], stats[:, sub:sub + 1], -1.0 / 2048.0))
            op("DVE", lambda v, sub=sub: v.tensor_scalar_mul(
                stats[:, 12 + sub:13 + sub], stats[:, 4 + sub:5 + sub], 1.0 / 2048.0))
            edge("DVE")
            op("DVE", lambda v, sub=sub: v.tensor_mul(
                stats[:, 16 + sub:17 + sub], stats[:, 8 + sub:9 + sub],
                stats[:, 8 + sub:9 + sub]))
            edge("DVE")
            op("DVE", lambda v, sub=sub: v.tensor_scalar(
                stats[:, 20 + sub:21 + sub], stats[:, 12 + sub:13 + sub],
                stats[:, 16 + sub:17 + sub], EPS, ALU.subtract, ALU.add),
               incs=[c_dv.inc()])
            op("SE", lambda s, sub=sub: s.activation(
                stats[:, 24 + sub:25 + sub], stats[:, 20 + sub:21 + sub], AF.Ln),
               waits=[W(c_dv)])
            edge("SE")
            op("SE", lambda s, sub=sub: s.activation(
                stats[:, 28 + sub:29 + sub], stats[:, 24 + sub:25 + sub], AF.Exp,
                scale=-0.5),
               incs=[c_se.inc()])
            op("DVE", lambda v, sub=sub: v.tensor_mul(
                stats[:, 8 + sub:9 + sub], stats[:, 8 + sub:9 + sub],
                stats[:, 28 + sub:29 + sub]),
               waits=[W(c_se)], incs=[c_dv.inc()])
            nb_dv = c_dv.n
            nbuf = tmp4 if sub % 2 == 0 else a_f
            wts = [W(c_dv, nb_dv)]
            if tr_pe[sub % 2]:
                wts.append(W(c_pe, tr_pe[sub % 2]))
            op("SE", lambda s, xa=xa, nb=nbuf, sub=sub: s.activation(
                nb[:, :], xa, AF.Identity, bias=stats[:, 8 + sub:9 + sub],
                scale=stats[:, 28 + sub:29 + sub]),
               waits=wts, incs=[c_se.inc()])
            se_norm = c_se.n
            for rnd in range(4):
                gr = ln_round[0]
                ln_round[0] += 1
                ps = ps_b if gr % 2 == 0 else ps_a
                for j in range(4):
                    e = rnd * 4 + j
                    dst = ps[:, j * 512:j * 512 + 128]
                    srcb = nbuf[:, e * 128:(e + 1) * 128]
                    w = []
                    if j == 0:
                        w = [W(c_se, se_norm)]
                        if ln_evac[gr % 2]:
                            w.append(W(c_dv, ln_evac[gr % 2]))
                        else:
                            w.append(st_["last_b"] if gr % 2 == 0 else st_["last_a"])
                    op("PE", lambda t, d=dst, s=srcb: t.transpose(d, s, idenb[:, :]),
                       waits=w, incs=([c_pe.inc()] if j == 3 else []))
                src3 = ps[:, :].rearrange("p (b c) -> p b c", c=512)[:, :, 0:128]
                dst3 = xn_fm[:, :].rearrange("p (e c) -> p e c", c=512)[
                    :, rnd * 4:(rnd + 1) * 4, sub * 128:(sub + 1) * 128]
                op("DVE", lambda v, s3=src3, d3=dst3: v.tensor_copy(d3, s3),
                   waits=[W(c_pe)], incs=[c_dv.inc()])
                ln_evac[gr % 2] = c_dv.n
            tr_pe[sub % 2] = c_pe.n
        st_["last_b"] = W(c_dv, ln_evac[0])
        st_["last_a"] = W(c_dv, ln_evac[1])
        st_["xn_ready"] = W(c_dv)

    hs_ready = [0] * 8

    def ph1(half):
        for ql in range(8):
            q = half * 8 + ql
            base_srl = wg_serial[0]
            wff_at = {}
            for hh in range(4):
                srl = base_srl + hh
                slot = srl % 4
                dg = wg_ring[:, slot * 2048:(slot + 1) * 2048].rearrange(
                    "p (h c) -> p h c", c=512)
                du = wu_ring[:, slot * 2048:(slot + 1) * 2048].rearrange(
                    "p (h c) -> p h c", c=512)
                wg_w = r_wg.load("SP", slot,
                                 lambda g, d=dg, s=wgb[q, hh]: g.dma_start(d, s),
                                 extra_waits=[cast_after[("wg", q // 2)]])
                wu_w = r_wu.load("SE", slot,
                                 lambda g, d=du, s=wub[q, hh]: g.dma_start(d, s),
                                 extra_waits=[cast_after[("wu", q // 2)]])
                wff_at[hh] = (wg_w, wu_w)
            wg_serial[0] += 4
            for h in range(16):
                hh, h4 = h // 4, h % 4
                slot = (base_srl + hh) % 4
                for m in range(4):
                    off = slot * 2048 + h4 * 512 + m * 128
                    lg = wg_ring[:, off:off + 128]
                    rhs = xn_fm[:, h * 512:(h + 1) * 512]
                    og = ps_a[:, m * 512:(m + 1) * 512]
                    wts = []
                    if h == 0 and m == 0:
                        wts = [st_["last_a"], st_["xn_ready"]]
                    if m == 0 and h4 == 0:
                        wts.append(wff_at[hh][0])
                    incs = [c_pe.inc()] if (h == 15 and m == 3) else []
                    op("PE", lambda t, o=og, l=lg, r=rhs, st=(h == 0), sp=(h == 15):
                       t.matmul(o, l, r, start=st, stop=sp), waits=wts, incs=incs)
            pe_g_done = c_pe.n
            op("SE", lambda s: s.activation(h_f[:, :], ps_a[:, :], AF.Silu),
               waits=[W(c_pe, pe_g_done), W(c_dv)], incs=[c_se.inc()])
            st_["last_a"] = W(c_se)
            for h in range(16):
                hh, h4 = h // 4, h % 4
                slot = (base_srl + hh) % 4
                for m in range(4):
                    off = slot * 2048 + h4 * 512 + m * 128
                    lu = wu_ring[:, off:off + 128]
                    rhs = xn_fm[:, h * 512:(h + 1) * 512]
                    ou = ps_b[:, m * 512:(m + 1) * 512]
                    wts = []
                    if h == 0 and m == 0:
                        wts = [st_["last_b"], st_["xn_ready"]]
                    if m == 0 and h4 == 0:
                        wts.append(wff_at[hh][1])
                    incs = []
                    if h4 == 3 and m == 3:
                        incs = [c_pe.inc()]
                    op("PE", lambda t, o=ou, l=lu, r=rhs, st=(h == 0), sp=(h == 15):
                       t.matmul(o, l, r, start=st, stop=sp), waits=wts, incs=incs)
                    if h4 == 3 and m == 3:
                        r_wg.free_at[slot] = c_pe.n
                        r_wu.free_at[slot] = c_pe.n
            pe_done = c_pe.n
            op("DVE", lambda v: v.tensor_mul(h_f[:, :], h_f[:, :], ps_b[:, :]),
               waits=[W(c_pe, pe_done), W(c_se)], incs=[c_dv.inc()])
            st_["last_b"] = W(c_dv)
            edge("DVE")
            op("DVE", lambda v: v.tensor_single_scalar(
                a_f[:, :].bitcast(mybir.dt.int32), h_f[:, :].bitcast(mybir.dt.int32),
                0x7FFFFFFF, ALU.bitwise_and))
            sl = [a_f[:, i * 512:(i + 1) * 512] for i in range(4)]
            tm = [tmp4[:, i * 512:(i + 1) * 512] for i in range(4)]
            mb = [mskf[:, i * 512:(i + 1) * 512] for i in range(3)]
            edge("DVE")
            op("DVE", lambda v: v.tensor_max(tm[0], sl[0], sl[1]))
            op("DVE", lambda v: v.tensor_max(tm[1], sl[2], sl[3]))
            op("DVE", lambda v: v.tensor_tensor(tm[2], sl[0], sl[1], ALU.min))
            op("DVE", lambda v: v.tensor_tensor(tm[3], sl[2], sl[3], ALU.min))
            edge("DVE")
            op("DVE", lambda v: v.tensor_tensor(tsel[:, :], tm[0], tm[1], ALU.min))
            op("DVE", lambda v: v.tensor_max(mb[2], tm[2], tm[3]))
            edge("DVE")
            op("DVE", lambda v: v.tensor_max(tsel[:, :], tsel[:, :], mb[2]))
            edge("DVE")
            bufs = [0, 1, 2, 0]

            def cmp_m(m):
                op("DVE", lambda v, m=m: v.tensor_tensor(
                    mb[bufs[m]], sl[m], tsel[:, :], ALU.is_ge))

            def mul_m(m, inc=False):
                c_loc = ql * 4 + m
                dsth = hs[:, c_loc * 512:(c_loc + 1) * 512]
                op("DVE", lambda v, m=m, d=dsth: v.tensor_mul(
                    d, h_f[:, m * 512:(m + 1) * 512], mb[bufs[m]]),
                   incs=([c_dv.inc()] if inc else []))

            cmp_m(0); cmp_m(1); cmp_m(2)
            edge("DVE")
            mul_m(0); mul_m(1)
            edge("DVE")
            cmp_m(3); mul_m(2)
            edge("DVE")
            mul_m(3, inc=True)
            hs_ready[ql] = c_dv.n

    def ph2(half):
        for pair in range(2):
            subs = (0, 1) if pair == 0 else (2, 3)
            yps = {subs[0]: ps_a, subs[1]: ps_b}
            for i in range(32):
                rowbase = half * 4096 + i * 128
                slot = wd_serial[0] % 4
                wd_serial[0] += 1
                dst = wd_ring[:, slot * 2048:(slot + 1) * 2048]
                wd_at = r_wd.load(
                    "SE" if slot % 2 == 0 else "SP", slot,
                    lambda s, d=dst, sc=wdb[rowbase:rowbase + 128, :]: s.dma_start(d, sc),
                    extra_waits=[cast_after[("wdq", half * 2 + i // 16)]])
                for sub in subs:
                    lhs = hs[:, i * 512 + sub * 128:i * 512 + sub * 128 + 128]
                    for ocol in range(4):
                        o = yps[sub][:, ocol * 512:(ocol + 1) * 512]
                        r = wd_ring[:, slot * 2048 + ocol * 512:slot * 2048 + (ocol + 1) * 512]
                        wts2 = []
                        if i == 0 and ocol == 0:
                            wts2.append(st_["last_a"] if sub == subs[0] else st_["last_b"])
                        if ocol == 0 and sub == subs[0]:
                            wts2.append(W(c_dv, hs_ready[i // 4]))
                            wts2.append(wd_at)
                        incs = []
                        if sub == subs[1] and ocol == 3:
                            incs = [c_pe.inc()]
                        op("PE", lambda t, o=o, l=lhs, r=r, st=(i == 0), sp=(i == 31):
                           t.matmul(o, l, r, start=st, stop=sp), waits=wts2, incs=incs)
                        if sub == subs[1] and ocol == 3:
                            r_wd.free_at[slot] = c_pe.n
            pe_y = c_pe.n
            for sub in subs:
                xa = x_res[:, sub * 2048:(sub + 1) * 2048]
                op("DVE", lambda v, xa=xa, p=yps[sub]: v.scalar_tensor_tensor(
                    xa, p[:, :], 0.5, xa, ALU.mult, ALU.add),
                   waits=[W(c_pe, pe_y)], incs=[c_dv.inc()])
                st_["xres_wr"][sub] = W(c_dv)
                if sub == subs[0]:
                    st_["last_a"] = W(c_dv)
                else:
                    st_["last_b"] = W(c_dv)

    def attn_emit(t_i):
        for h in range(16):
            slot = h % 4
            dst = wkv_ring[:, slot * 1024:(slot + 1) * 1024]
            wat_w = r_wkv.load(
                "SP", slot,
                lambda g, d=dst, s=wkvb[h * 128:(h + 1) * 128, :]: g.dma_start(d, s),
                extra_waits=[cast_after["wkv"]])
            for g_ in range(4):
                lk = wkv_ring[:, slot * 1024 + g_ * 128:slot * 1024 + (g_ + 1) * 128]
                rhs = xn_fm[:, h * 512:(h + 1) * 512]
                o = ps_a[:, g_ * 512:(g_ + 1) * 512]
                wts2 = []
                if g_ == 0:
                    wts2.append(wat_w)
                    if h == 0:
                        wts2 += [st_["last_a"], st_["last_b"]]
                op("PE", lambda t, o=o, l=lk, r=rhs, st=(h == 0), sp=(h == 15):
                   t.matmul(o, l, r, start=st, stop=sp), waits=wts2)
            for sub in range(4):
                lx = xn_fm[:, h * 512 + sub * 128:h * 512 + sub * 128 + 128]
                rv = wkv_ring[:, slot * 1024 + 512:slot * 1024 + 1024]
                o = ps_b[:, sub * 512:(sub + 1) * 512]
                op("PE", lambda t, o=o, l=lx, r=rv, st=(h == 0), sp=(h == 15):
                   t.matmul(o, l, r, start=st, stop=sp),
                   incs=([c_pe.inc()] if sub == 3 else []))
                if sub == 3:
                    r_wkv.free_at[slot] = c_pe.n
        pe_kv = c_pe.n
        for sub in range(4):
            dst = v_ctx[:, (5 + sub) * 512:(6 + sub) * 512]
            o = ps_b[:, sub * 512:(sub + 1) * 512]
            op("DVE", lambda v, d=dst, o=o: v.tensor_add(d, o, bkvv_bc[:, :]),
               waits=([W(c_pe, pe_kv)] if sub == 0 else []),
               incs=([c_dv.inc()] if sub == 3 else []))
        st_["last_b"] = W(c_dv)
        prev_rot_read = W(c_dv, 0)
        for g_ in range(4):
            kps = ps_a[:, g_ * 512:(g_ + 1) * 512]
            wts = [W(c_pe, pe_kv)]
            if g_ > 0:
                wts.append(prev_rot_read)
                wts.append(W(c_pe))
            op("SE", lambda s, o=kps, g_=g_: s.activation(
                qb_f[:, 0:512], o, AF.Identity, bias=bkvk_s[:, g_:g_ + 1]),
               waits=wts, incs=[c_se.inc()])
            se_kb = c_se.n
            op("PE", lambda t: t.matmul(ps_b[:, 0:512], propeb[:, :], qb_f[:, 0:512],
                                        start=True, stop=True),
               waits=[W(c_se, se_kb), st_["last_b"], W(c_ldx, st_["cos_ready"])],
               incs=[c_pe.inc()])
            pe_rot = c_pe.n
            op("DVE", lambda v: v.tensor_mul(rt1[:, :], qb_f[:, 0:512], cos_t[:, :]),
               waits=[W(c_pe, pe_rot)])
            op("DVE", lambda v: v.tensor_mul(rt2[:, :], ps_b[:, 0:512], sin_t[:, :]))
            edge("DVE")
            kdst = k_ctx[:, g_ * 1152 + 640:g_ * 1152 + 1152]
            op("DVE", lambda v, d=kdst: v.tensor_add(d, rt1[:, :], rt2[:, :]),
               incs=[c_dv.inc()])
            prev_rot_read = W(c_dv)
            st_["last_b"] = W(c_dv)
        st_["last_a"] = W(c_dv)
        dv_kctx = c_dv.n

        qbuf_dv = [0, 0]
        qbank_se = [0, 0, 0, 0]
        for e in range(16):
            slot = e % 2
            dq = wq_ring[:, slot * 2048:(slot + 1) * 2048].rearrange(
                "p (h c) -> p h c", c=128)
            wat_w = r_wq.load("SP", slot,
                              lambda g, d=dq, s=wqb[e]: g.dma_start(d, s),
                              extra_waits=[cast_after["wq"]])
            bank = e % 4
            o = ps_a[:, bank * 512:(bank + 1) * 512]
            for h in range(16):
                l = wq_ring[:, slot * 2048 + h * 128:slot * 2048 + (h + 1) * 128]
                rhs = xn_fm[:, h * 512:(h + 1) * 512]
                wts2 = []
                if h == 0:
                    wts2.append(wat_w)
                    if e == 0:
                        wts2.append(st_["last_a"])
                    if qbank_se[bank]:
                        wts2.append(W(c_se, qbank_se[bank]))
                op("PE", lambda t, o=o, l=l, r=rhs, st=(h == 0), sp=(h == 15):
                   t.matmul(o, l, r, start=st, stop=sp), waits=wts2,
                   incs=([c_pe.inc()] if h == 15 else []))
                if h == 15:
                    r_wq.free_at[slot] = c_pe.n
            pe_q = c_pe.n
            qb = qb_f[:, (e % 2) * 512:(e % 2) * 512 + 512]
            wts = [W(c_pe, pe_q)]
            if qbuf_dv[e % 2]:
                wts.append(W(c_dv, qbuf_dv[e % 2]))
            op("SE", lambda s, o=o, qb=qb, e=e: s.activation(
                qb, o, AF.Identity, bias=bq_s[:, e:e + 1], scale=SCALE),
               waits=wts, incs=[c_se.inc()])
            qbank_se[bank] = c_se.n
            op("PE", lambda t, qb=qb: t.matmul(ps_b[:, 0:512], propeb[:, :], qb,
                                               start=True, stop=True),
               waits=[W(c_se), W(c_dv)], incs=[c_pe.inc()])
            pe_rot = c_pe.n
            op("DVE", lambda v, qb=qb: v.tensor_mul(rt1[:, :], qb, cos_t[:, :]),
               waits=[W(c_pe, pe_rot)])
            op("DVE", lambda v: v.tensor_mul(rt2[:, :], ps_b[:, 0:512], sin_t[:, :]))
            edge("DVE")
            qdst = q_fm[:, e * 512:(e + 1) * 512]
            op("DVE", lambda v, d=qdst: v.tensor_add(d, rt1[:, :], rt2[:, :]),
               incs=[c_dv.inc()])
            qbuf_dv[e % 2] = c_dv.n
        dv_q = c_dv.n
        se_qb_final = c_se.n
        st_["rope_done"] = W(c_dv, dv_q)

        if t_i == 0:
            blocks = list(range(5, 9))
        elif t_i == 1:
            blocks = list(range(1, 9))
        else:
            blocks = list(range(0, 9))

        prev_k2_pe = 0
        for g2 in range(4):
            wts = [W(c_dv, dv_kctx)]
            if prev_k2_pe:
                wts.append(W(c_pe, prev_k2_pe))
            op("SE", lambda s, g2=g2: s.activation(
                k2b[:, :], k_ctx[:, g2 * 1152:(g2 + 1) * 1152], AF.Square),
               waits=wts, incs=[c_se.inc()])
            se_k2 = c_se.n
            spans = [(0, 512, ps_b[0:1, 1024:1536]),
                     (512, 1024, ps_b[0:1, 1536:2048]),
                     (1024, 1152, ps_b[0:1, 512:640])]
            for si, (c0, c1, o) in enumerate(spans):
                wts2 = [W(c_se, se_k2)]
                if si == 0:
                    wts2.append(W(c_dv))
                    if g2 == 0:
                        wts2.append(st_["last_b"])
                op("PE", lambda t, o=o, c0=c0, c1=c1: t.matmul(
                    o, onecb[:, :], k2b[:, c0:c1], start=True, stop=True),
                   waits=wts2, incs=([c_pe.inc()] if si == 2 else []))
            prev_k2_pe = c_pe.n
            op("DVE", lambda v: v.tensor_reduce(srow[0:1, 4:5], ps_b[0:1, 1024:1536],
                                                AX.X, ALU.max),
               waits=[W(c_pe, prev_k2_pe)])
            op("DVE", lambda v: v.tensor_reduce(srow[0:1, 5:6], ps_b[0:1, 1536:2048],
                                                AX.X, ALU.max))
            op("DVE", lambda v: v.tensor_reduce(srow[0:1, 6:7], ps_b[0:1, 512:640],
                                                AX.X, ALU.max))
            edge("DVE")
            op("DVE", lambda v: v.tensor_max(srow[0:1, 4:5], srow[0:1, 4:5],
                                             srow[0:1, 5:6]))
            edge("DVE")
            op("DVE", lambda v, g2=g2: v.tensor_max(
                srow[0:1, g2:g2 + 1], srow[0:1, 4:5], srow[0:1, 6:7]),
               incs=[c_dv.inc()])
        c2_dv = c_dv.n
        st_["last_b"] = W(c_dv)

        nm_dv = {}
        sqrt_se = {}
        nmcopy_dv = {}

        def stats_emit(hd):
            g2 = hd // 4
            pp = hd % 2
            qsl2 = q_fm[:, hd * 512:(hd + 1) * 512]
            op("SE", lambda s, q=qsl2: s.activation(q2b[:, :], q, AF.Square),
               waits=[W(c_dv, dv_q)], incs=[c_se.inc()])
            se_q2 = c_se.n
            wts = [W(c_se, se_q2)]
            if hd >= 1 and (hd - 1) in sqrt_se:
                wts.append(W(c_se, sqrt_se[hd - 1]))
            else:
                wts.append(W(c_dv, dv_q))
            op("PE", lambda t: t.matmul(ps_b[0:1, 0:512], onecb[:, :], q2b[:, :],
                                        start=True, stop=True),
               waits=wts, incs=[c_pe.inc()])
            pe_q2 = c_pe.n
            ar = arow[0:1, pp * 512:pp * 512 + 512]
            op("SE", lambda s, g2=g2, ar=ar: s.activation(
                ar, ps_b[0:1, 0:512], AF.Sqrt, scale=srow[0:1, g2:g2 + 1]),
               waits=[W(c_pe, pe_q2), W(c_dv, c2_dv)], incs=[c_se.inc()])
            sqrt_se[hd] = c_se.n
            nr = nmrow[0:1, pp:pp + 1]
            op("DVE", lambda v, ar=ar, nr=nr: v.tensor_reduce(
                nr, ar, AX.X, ALU.max, negate=True),
               waits=[W(c_se)], incs=[c_dv.inc()])
            dv_nr = c_dv.n
            wts = [W(c_dv, dv_nr)]
            if (hd - 1) in nmcopy_dv:
                wts.append(W(c_dv, nmcopy_dv[hd - 1]))
            op("PE", lambda t, nr=nr: t.matmul(ps_b[:, 1536:1537], onerb[:, :], nr,
                                               start=True, stop=True),
               waits=wts, incs=[c_pe.inc()])
            pe_nm = c_pe.n
            op("DVE", lambda v, pp=pp: v.tensor_copy(nm128[:, pp:pp + 1],
                                                     ps_b[:, 1536:1537]),
               waits=[W(c_pe, pe_nm)], incs=[c_dv.inc()])
            nm_dv[hd] = c_dv.n
            nmcopy_dv[hd] = c_dv.n

        stats_emit(0)

        nb = len(blocks)
        triples = [blocks[i:i + 3] for i in range(0, nb, 3)]
        ntr = len(triples)
        exp3_done = {}
        mask_dv = {}
        recip_dv = {}
        ymul_dv = {}

        def emit_triple(hd, ti):
            g2 = hd // 4
            pp = hd % 2
            qsl2 = q_fm[:, hd * 512:(hd + 1) * 512]
            trp = triples[ti]
            for j, b in enumerate(trp):
                o = ps_a[:, j * 512:(j + 1) * 512]
                lk = k_ctx[:, g2 * 1152 + b * 128:g2 * 1152 + (b + 1) * 128]
                wts2 = []
                if j == 0:
                    if ti == 0 and hd == 0:
                        wts2 = [W(c_se, se_qb_final), W(c_dv, dv_q), st_["last_a"]]
                    else:
                        prev = exp3_done.get((hd, ti - 1)) or                             exp3_done.get((hd - 1, ntr - 1))
                        if prev:
                            wts2 = [W(c_se, prev)]
                op("PE", lambda t, o=o, l=lk, q=qsl2: t.matmul(
                    o, l, q, start=True, stop=True),
                   waits=wts2, incs=([c_pe.inc()] if j == len(trp) - 1 else []))
            pe_s = c_pe.n
            b0 = trp[0]
            w = len(trp) * 512
            pdst = p_sb[:, b0 * 512:b0 * 512 + w]
            op("SE", lambda s, d=pdst, w=w, pp=pp: s.activation(
                d, ps_a[:, 0:w], AF.Exp, bias=nm128[:, pp:pp + 1]),
               waits=[W(c_pe, pe_s), W(c_dv, nm_dv[hd])], incs=[c_se.inc()])
            exp3_done[(hd, ti)] = c_se.n
            for b in trp:
                pd = p_sb[:, b * 512:(b + 1) * 512]
                if b == 0:
                    msl = mmetab[:, :]
                elif t_i == 1 and b == 1:
                    msl = mb1b[:, :]
                else:
                    msl = stripb[:, 1024 - 128 * b:1536 - 128 * b]
                op("DVE", lambda v, d=pd, m=msl: v.tensor_mul(d, d, m),
                   waits=[W(c_se)], incs=[c_dv.inc()])
                mask_dv[(hd, b)] = c_dv.n

        def emit_lav(hd, ti):
            g2 = hd // 4
            base = sum(len(triples[t]) for t in range(ti))
            for k, b in enumerate(triples[ti]):
                bi = base + k
                psl = p_sb[:, b * 512:(b + 1) * 512]
                first, last = (bi == 0), (bi == nb - 1)
                wts2 = [W(c_dv, mask_dv[(hd, b)])]
                if first and hd > 0:
                    wts2.append(W(c_dv, recip_dv[hd - 1]))
                op("PE", lambda t, p=psl, st=first, sp=last: t.matmul(
                    ps_b[0:1, 1024:1536], onecb[:, :], p, start=st, stop=sp),
                   waits=wts2)
                lv = v_ctx[:, b * 512 + g2 * 128:b * 512 + (g2 + 1) * 128]
                wts3 = []
                if first and hd > 0:
                    wts3.append(W(c_dv, ymul_dv[hd - 1]))
                op("PE", lambda t, l=lv, p=psl, st=first, sp=last: t.matmul(
                    ps_a[:, 1536:2048], l, p, start=st, stop=sp),
                   waits=wts3, incs=([c_pe.inc()] if last else []))

        def finalize(hd):
            pe_lav = c_pe.n
            op("DVE", lambda v: v.reciprocal(linv[:, :], ps_b[0:1, 1024:1536]),
               waits=[W(c_pe, pe_lav)], incs=[c_dv.inc()])
            recip_dv[hd] = c_dv.n
            op("PE", lambda t: t.matmul(ps_b[:, 512:1024], onerb[:, :], linv[:, :],
                                        start=True, stop=True),
               waits=[W(c_dv)], incs=[c_pe.inc()])
            pe_rb = c_pe.n
            op("DVE", lambda v: v.tensor_copy(recip_bc[:, :], ps_b[:, 512:1024]),
               waits=[W(c_pe, pe_rb)])
            edge("DVE")
            ydst = q_fm[:, hd * 512:(hd + 1) * 512]
            op("DVE", lambda v, d=ydst: v.tensor_mul(d, ps_a[:, 1536:2048],
                                                     recip_bc[:, :]),
               incs=[c_dv.inc()])
            ymul_dv[hd] = c_dv.n
            st_["last_a"] = W(c_dv)

        for hd_i in range(16):
            for ti in range(ntr):
                emit_triple(hd_i, ti)
                if ti == ntr - 1 and hd_i + 1 < 16:
                    stats_emit(hd_i + 1)
                if ti >= 1:
                    emit_lav(hd_i, ti - 1)
                elif hd_i > 0:
                    emit_lav(hd_i - 1, ntr - 1)
                    finalize(hd_i - 1)
        emit_lav(15, ntr - 1)
        finalize(15)
        dv_y = c_dv.n

        pe_done = c_pe.n
        for g_ in range(4):
            src = k_ctx[:, g_ * 1152 + 640:g_ * 1152 + 1152]
            dst = k_ctx[:, g_ * 1152 + 128:g_ * 1152 + 640]
            op("DVE", lambda v, d=dst, s=src: v.tensor_copy(d, s),
               waits=([W(c_pe, pe_done)] if g_ == 0 else []))
        for b in range(4):
            src = v_ctx[:, (5 + b) * 512:(6 + b) * 512]
            dst = v_ctx[:, (1 + b) * 512:(2 + b) * 512]
            op("DVE", lambda v, d=dst, s=src: v.tensor_copy(d, s))
        if t_i == 0:
            for g_ in range(4):
                src = k_ctx[:, g_ * 1152 + 640:g_ * 1152 + 656]
                dst = k_ctx[:, g_ * 1152:g_ * 1152 + 16]
                op("DVE", lambda v, d=dst, s=src: v.tensor_copy(d, s))
            op("DVE", lambda v: v.tensor_copy(v_ctx[0:16, 0:512],
                                              v_ctx[0:16, 5 * 512:6 * 512]))
        op("DVE", lambda v: v.tensor_copy(stats[:, 32:33], stats[:, 33:34]),
           incs=[c_dv.inc()])

        for pairi in range(2):
            subs = (0, 1) if pairi == 0 else (2, 3)
            yps = {subs[0]: ps_a, subs[1]: ps_b}
            for e in range(16):
                slot = e % 2
                dst = wo_ring[:, slot * 2048:(slot + 1) * 2048]
                wat_w = r_wo.load(
                    "SP", slot,
                    lambda g, d=dst, s=wob[e * 128:(e + 1) * 128, :]: g.dma_start(d, s),
                    extra_waits=[cast_after["wo"]])
                for sub in subs:
                    lhs = q_fm[:, e * 512 + sub * 128:e * 512 + sub * 128 + 128]
                    for ocol in range(4):
                        o = yps[sub][:, ocol * 512:(ocol + 1) * 512]
                        r = wo_ring[:, slot * 2048 + ocol * 512:slot * 2048 + (ocol + 1) * 512]
                        wts2 = []
                        if e == 0 and ocol == 0:
                            wts2.append(st_["last_a"] if sub == subs[0] else st_["last_b"])
                            wts2.append(W(c_dv, dv_y))
                        if ocol == 0 and sub == subs[0]:
                            wts2.append(wat_w)
                        incs = []
                        if sub == subs[1] and ocol == 3:
                            incs = [c_pe.inc()]
                        op("PE", lambda t, o=o, l=lhs, r=r, st=(e == 0), sp=(e == 15):
                           t.matmul(o, l, r, start=st, stop=sp), waits=wts2, incs=incs)
                        if sub == subs[1] and ocol == 3:
                            r_wo.free_at[slot] = c_pe.n
            pe_o = c_pe.n
            for sub in subs:
                xa = x_res[:, sub * 2048:(sub + 1) * 2048]
                p = yps[sub]
                op("DVE", lambda v, xa=xa, p=p: v.tensor_add(xa, xa, p[:, :]),
                   waits=[W(c_pe, pe_o)])
                edge("DVE")
                op("DVE", lambda v, xa=xa: v.tensor_add(xa, xa, bo_bc[:, :]),
                   incs=[c_dv.inc()])
                st_["xres_wr"][sub] = W(c_dv)
                if sub == subs[0]:
                    st_["last_a"] = W(c_dv)
                else:
                    st_["last_b"] = W(c_dv)


    for t_i in range(n_tiles):
        for sub in range(4):
            dst = x_res[:, sub * 2048:(sub + 1) * 2048]
            src = x_h[t_i * 512 + sub * 128:t_i * 512 + (sub + 1) * 128, :]
            wts = [W(c_outs[sub], st_["out_done"][sub])] if t_i > 0 else (
                [W(c_ldx)] if sub == 0 else [])
            op("POOL", lambda g, d=dst, s=src: g.dma_start(d, s),
               waits=wts, incs=[c_xins[sub].inc()])
        wts = [st_["rope_done"]] if t_i > 0 else []
        op("POOL", lambda g, t=t_i: g.dma_start(cos_t[:, :],
                                                cos_h[:, t * 512:(t + 1) * 512]),
           waits=wts, incs=[c_ldx.inc()])
        op("POOL", lambda g, t=t_i: g.dma_start(sin_t[:, :],
                                                sin_h[:, t * 512:(t + 1) * 512]),
           incs=[c_ldx.inc()])
        st_["cos_ready"] = c_ldx.n
        for sub in range(4):
            st_["xres_wr"][sub] = W(c_xins[sub], c_xins[sub].n)

        ln_emit()
        ph1(0); ph2(0); ph1(1); ph2(1)
        r_wkv.bump_all(c_pe.n); r_wq.bump_all(c_pe.n); r_wo.bump_all(c_pe.n)
        ln_emit()
        attn_emit(t_i)
        r_wg.bump_all(c_pe.n); r_wu.bump_all(c_pe.n)
        ln_emit()
        ph1(0); ph2(0); ph1(1); ph2(1)
        r_wkv.bump_all(c_pe.n); r_wq.bump_all(c_pe.n); r_wo.bump_all(c_pe.n)
        for sub in range(4):
            src = x_res[:, sub * 2048:(sub + 1) * 2048]
            dst = out_h[t_i * 512 + sub * 128:t_i * 512 + (sub + 1) * 128, :]
            op("POOL", lambda g, d=dst, s=src: g.dma_start(d, s),
               waits=[st_["xres_wr"][sub]], incs=[c_outs[sub].inc()])
            st_["out_done"][sub] = c_outs[sub].n

    final_outs = [c.n for c in c_outs]

    with contextlib.ExitStack() as stk:
        for c in CTRS:
            if c.name != "wat_dummy":
                c.h = stk.enter_context(nc.semaphore(c.name))
        blk = stk.enter_context(nc.Block())

        def replay(name, eng):
            hw = {}
            for waits, fn, incs in plan[name]:
                for (ctr, val) in waits:
                    if val <= 0 or hw.get(ctr.name, 0) >= val:
                        continue
                    eng.wait_ge(ctr.h, val)
                    hw[ctr.name] = val
                instr = fn(eng)
                for (ctr, amt) in incs:
                    instr = instr.then_inc(ctr.h, amt)

        @blk.tensor
        def _(t):
            replay("PE", t)

        @blk.vector
        def _(v):
            replay("DVE", v)

        @blk.scalar
        def _(s):
            replay("SE", s)

        @blk.sync
        def _(s):
            replay("SP", s)

        @blk.gpsimd
        def _(g):
            replay("POOL", g)
            for c, v in zip(c_outs, final_outs):
                if v:
                    g.wait_ge(c.h, v)

    return nc


def _perm():
    c = np.arange(II)
    return 512 * (c // 512) + 4 * (c % 128) + (c % 512) // 128


def _host_constants():
    import ml_dtypes
    inv = 1.0 / 10000.0 ** (np.arange(0, HD, 2, dtype=np.float64) / HD)
    ang = np.arange(S, dtype=np.float64)[None, :] * inv[:, None]
    cosfm = np.ascontiguousarray(np.repeat(np.cos(ang), 2, axis=0).astype(np.float32))
    sinfm = np.ascontiguousarray(np.repeat(np.sin(ang), 2, axis=0).astype(np.float32))
    k = np.arange(128)[:, None]
    u = np.arange(1408)[None, :]
    band = ((u - 384 - k >= 0) & (u - 384 - k <= 511)).astype(np.float32)
    strip = band.astype(ml_dtypes.bfloat16)
    b1 = band[:, 896:1408].copy()
    b1[k[:, 0] < 16, :] = 1.0
    maskb1 = b1.astype(ml_dtypes.bfloat16)
    maskmeta = np.zeros((128, 512), np.float32)
    maskmeta[:16, :] = 1.0
    maskmeta = maskmeta.astype(ml_dtypes.bfloat16)
    iden = np.eye(128, dtype=np.float32)
    pr = np.zeros((128, 128), np.float32)
    for i in range(64):
        pr[2 * i + 1, 2 * i] = -1.0
        pr[2 * i, 2 * i + 1] = 1.0
    onec = np.ones((128, 1), ml_dtypes.bfloat16)
    oner = np.ones((1, 128), np.float32)
    return dict(cosfm=cosfm, sinfm=sinfm, strip=strip, maskb1=maskb1,
                maskmeta=maskmeta, idenf=iden, prope=pr, onec_bf=onec,
                oner_f=oner)


def _tile_weights(Wq, w_gate, w_up, w_down):
    perm = _perm()
    wg_p = w_gate[:, perm]
    wu_p = w_up[:, perm]
    wd_p = np.ascontiguousarray(w_down[perm, :])
    a = wg_p.reshape(4, 4, 128, 16, 512)
    wg_t = np.ascontiguousarray(np.transpose(a, (3, 0, 2, 1, 4)))
    a = wu_p.reshape(4, 4, 128, 16, 512)
    wu_t = np.ascontiguousarray(np.transpose(a, (3, 0, 2, 1, 4)))
    a = Wq.reshape(16, 128, 16, 128)
    wq_t = np.ascontiguousarray(np.transpose(a, (2, 1, 0, 3)))
    return wg_t, wu_t, wd_p, wq_t


def make_inputs_common(Wq, bq, Wkv, bkv, Wo, bo, wg, wu, wd):
    wg_t, wu_t, wd_p, wq_t = _tile_weights(Wq, wg, wu, wd)
    consts = _host_constants()
    return {
        "wg_t": wg_t, "wu_t": wu_t, "wd_t": wd_p, "wq_t": wq_t,
        "wkv_t": np.ascontiguousarray(Wkv), "wo_t": np.ascontiguousarray(Wo),
        "bq": np.ascontiguousarray(bq.reshape(16, 128)),
        "bkv": np.ascontiguousarray(bkv.reshape(8, 128)),
        "bkvv": np.ascontiguousarray(bkv[512:].reshape(1, 512)),
        "bo_r": np.ascontiguousarray(bo.reshape(1, E)),
        **consts,
    }


def kernel(L0_attn_x, L0_attn_Wq, L0_attn_bq, L0_attn_Wkv, L0_attn_bkv,
           L0_attn_Wo, L0_attn_bo, L0_ffn_w_gate, L0_ffn_w_up, L0_ffn_w_down):
    x = np.asarray(L0_attn_x, dtype=np.float32)
    common = make_inputs_common(
        np.asarray(L0_attn_Wq, np.float32), np.asarray(L0_attn_bq, np.float32),
        np.asarray(L0_attn_Wkv, np.float32), np.asarray(L0_attn_bkv, np.float32),
        np.asarray(L0_attn_Wo, np.float32), np.asarray(L0_attn_bo, np.float32),
        np.asarray(L0_ffn_w_gate, np.float32), np.asarray(L0_ffn_w_up, np.float32),
        np.asarray(L0_ffn_w_down, np.float32))
    nc = build_nc(n_tiles=8)
    in_maps = [dict(common, x=np.ascontiguousarray(x[b])) for b in range(8)]
    res = run_bass_kernel_spmd(nc, in_maps, core_ids=list(range(8)))
    return np.stack([res.results[b]["out"] for b in range(8)], axis=0)


# version 15
# speedup vs baseline: 1.0974x; 1.0974x over previous
import numpy as np
import contextlib
import concourse.bass as bass
import concourse.mybir as mybir
from concourse.bass_utils import run_bass_kernel_spmd

F32 = mybir.dt.float32
BF16 = mybir.dt.bfloat16
AF = mybir.ActivationFunctionType
ALU = mybir.AluOpType
AX = mybir.AxisListType

S, E, T, HD, NH, NKV, II = 4096, 2048, 512, 128, 16, 4, 8192
SCALE = float(1.0 / np.sqrt(128.0))
EPS = 1e-5
ENGINES = ("PE", "DVE", "SE", "SP", "POOL")


def build_nc(n_tiles=8):
    nc = bass.Bass("TRN2", target_bir_lowering=False, debug=False)

    x_h = nc.dram_tensor("x", [S, E], F32, kind="ExternalInput").ap()
    wg_h = nc.dram_tensor("wg_t", [16, 4, 128, 4, 512], F32, kind="ExternalInput").ap()
    wu_h = nc.dram_tensor("wu_t", [16, 4, 128, 4, 512], F32, kind="ExternalInput").ap()
    wd_h = nc.dram_tensor("wd_t", [II, E], F32, kind="ExternalInput").ap()
    wq_h = nc.dram_tensor("wq_t", [16, 128, 16, 128], F32, kind="ExternalInput").ap()
    wkv_h = nc.dram_tensor("wkv_t", [E, 1024], F32, kind="ExternalInput").ap()
    wo_h = nc.dram_tensor("wo_t", [E, E], F32, kind="ExternalInput").ap()
    bq_h = nc.dram_tensor("bq", [16, 128], F32, kind="ExternalInput").ap()
    bkv_h = nc.dram_tensor("bkv", [8, 128], F32, kind="ExternalInput").ap()
    bkvv_h = nc.dram_tensor("bkvv", [1, 512], F32, kind="ExternalInput").ap()
    bo_h = nc.dram_tensor("bo_r", [1, E], F32, kind="ExternalInput").ap()
    cos_h = nc.dram_tensor("cosfm", [128, S], F32, kind="ExternalInput").ap()
    sin_h = nc.dram_tensor("sinfm", [128, S], F32, kind="ExternalInput").ap()
    strip_h = nc.dram_tensor("strip", [128, 1408], BF16, kind="ExternalInput").ap()
    mb1_h = nc.dram_tensor("maskb1", [128, 512], BF16, kind="ExternalInput").ap()
    mmeta_h = nc.dram_tensor("maskmeta", [128, 512], BF16, kind="ExternalInput").ap()
    iden_h = nc.dram_tensor("idenf", [128, 128], F32, kind="ExternalInput").ap()
    prope_h = nc.dram_tensor("prope", [128, 128], F32, kind="ExternalInput").ap()
    onec_h = nc.dram_tensor("onec_bf", [128, 1], BF16, kind="ExternalInput").ap()
    oner_h = nc.dram_tensor("oner_f", [1, 128], F32, kind="ExternalInput").ap()
    out_h = nc.dram_tensor("out", [S, E], F32, kind="ExternalOutput").ap()

    wgb = nc.dram_tensor("wg_bf", [16, 4, 128, 4, 512], BF16, kind="Internal").ap()
    wub = nc.dram_tensor("wu_bf", [16, 4, 128, 4, 512], BF16, kind="Internal").ap()
    wdb = nc.dram_tensor("wd_bf", [II, E], BF16, kind="Internal").ap()
    wqb = nc.dram_tensor("wq_bf", [16, 128, 16, 128], BF16, kind="Internal").ap()
    wkvb = nc.dram_tensor("wkv_bf", [E, 1024], BF16, kind="Internal").ap()
    wob = nc.dram_tensor("wo_bf", [E, E], BF16, kind="Internal").ap()

    A = nc.alloc_sbuf_tensor
    x_res = A("x_res", [128, 4 * 2048], F32)
    xn_fm = A("xn_fm", [128, 16 * 512], BF16)
    k_ctx = A("k_ctx", [128, 4 * 1152], BF16)
    v_ctx = A("v_ctx", [128, 9 * 512], BF16)
    cos_t = A("cos_t", [128, 512], F32)
    sin_t = A("sin_t", [128, 512], F32)
    bo_bc = A("bo_bc", [128, 2048], F32)
    bkvv_bc = A("bkvv_bc", [128, 512], F32)
    stripb = A("stripb", [128, 1408], BF16)
    mb1b = A("mb1b", [128, 512], BF16)
    mmetab = A("mmetab", [128, 512], BF16)
    idenb = A("idenb", [128, 128], F32)
    propeb = A("propeb", [128, 128], F32)
    onecb = A("onecb", [128, 1], BF16)
    onerb = A("onerb", [1, 128], F32)
    bq_s = A("bq_s", [128, 16], F32)
    bkvk_s = A("bkvk_s", [128, 8], F32)
    bqraw = A("bqraw", [16, 128], F32)
    bkraw = A("bkraw", [8, 128], F32)
    stats = A("stats", [128, 36], F32)
    srow = A("srow", [1, 16], F32)
    arow = A("arow", [1, 1024], F32)
    nmrow = A("nmrow", [1, 2], F32)
    nm128 = A("nm128", [128, 2], F32)
    linv = A("linv", [1, 512], F32)
    h_f = A("h_f", [128, 2048], F32)
    a_f = A("a_f", [128, 2048], F32)
    tmp4 = A("tmp4", [128, 2048], F32)
    tsel = A("tsel", [128, 512], F32)
    mskf = A("mskf", [128, 3 * 512], F32)
    wd_ring = A("wd_ring", [128, 4 * 2048], BF16)
    arena = A("arena", [128, 16384], F32)
    arena_bf = arena.bitcast(BF16)
    hs = arena_bf[:, 0:16384]
    wg_ring = arena_bf[:, 16384:24576]
    wu_ring = arena_bf[:, 24576:32768]
    q_fm = arena_bf[:, 0:8192]
    p_sb = arena_bf[:, 8192:12800]
    qb_f = arena[:, 6400:7424]
    rt1 = arena[:, 7424:7936]
    rt2 = arena[:, 7936:8448]
    recip_bc = arena[:, 8448:8960]
    k2b = arena_bf[:, 17920:19072]
    q2b = arena_bf[:, 19072:19584]
    wkv_ring = arena_bf[:, 19584:23680]
    wq_ring = arena_bf[:, 23680:27776]
    wo_ring = arena_bf[:, 27776:31872]

    ps_a = nc.alloc_psum_tensor("ps_a", [128, 2048], F32)
    ps_b = nc.alloc_psum_tensor("ps_b", [128, 2048], F32)

    plan = {e: [] for e in ENGINES}

    class Ctr:
        __slots__ = ("name", "step", "n", "h")

        def __init__(self, name, step):
            self.name, self.step, self.n, self.h = name, step, 0, None

        def inc(self, k=1):
            self.n += k * self.step
            return (self, k * self.step)

    c_ldx = Ctr("ldx", 16)
    c_outs = [Ctr(f"ldo{i}", 16) for i in range(4)]
    c_xins = [Ctr(f"ldi{i}", 16) for i in range(4)]
    c_pe = Ctr("pe", 1)
    c_dv = Ctr("dv", 1)
    c_se = Ctr("se", 1)
    CTRS = [c_ldx, c_pe, c_dv, c_se] + c_outs + c_xins

    class Ring:

        def __init__(self, name, nslots):
            self.ctrs = [Ctr(f"{name}{i}", 16) for i in range(nslots)]
            self.free_at = [0] * nslots
            self.n = nslots
            CTRS.extend(self.ctrs)

        def load(self, eng, slot, fn, extra_waits=()):
            wts = list(extra_waits)
            if self.free_at[slot]:
                wts.append(W(c_pe, self.free_at[slot]))
            op(eng, fn, waits=wts, incs=[self.ctrs[slot].inc()])
            return W(self.ctrs[slot])

        def bump_all(self, val):
            for i in range(self.n):
                self.free_at[i] = max(self.free_at[i], val)

    r_wg = Ring("rwg", 4)
    r_wu = Ring("rwu", 4)
    r_wd = Ring("rwd", 4)
    r_wkv = Ring("rkv", 4)
    r_wq = Ring("rwq", 2)
    r_wo = Ring("rwo", 2)

    pending_edge = {}

    def op(eng, fn, waits=(), incs=()):
        pe_ = pending_edge.pop(eng, None)
        w = tuple(waits) + ((pe_,) if pe_ else ())
        plan[eng].append((w, fn, tuple(incs)))

    def W(ctr, val=None):
        return (ctr, ctr.n if val is None else val)

    def dma(dst, src):
        return lambda g, d=dst, s=src: g.dma_start(d, s)

    def edge(eng):
        c = {"DVE": c_dv, "SE": c_se, "PE": c_pe}[eng]
        waits, fn, incs = plan[eng][-1]
        for (ctr, amt) in incs:
            if ctr is c:
                pending_edge[eng] = (c, c.n)
                return
        assert not incs, "op already has a different-counter inc"
        plan[eng][-1] = (waits, fn, incs + (c.inc(),))
        pending_edge[eng] = (c, c.n)

    st_ = {
        "last_a": W(c_dv, 0), "last_b": W(c_dv, 0),
        "xres_wr": [W(c_dv, 0)] * 4,
        "rope_done": W(c_dv, 0),
        "cos_ready": 0,
        "out_done": [0, 0, 0, 0],
        "xn_ready": W(c_dv, 0),
        "tr_pe": [0, 0],
    }
    wg_serial = [0]
    wd_serial = [0]

    for dst, src in ((stripb[:, :], strip_h[:, :]), (mb1b[:, :], mb1_h[:, :]),
                     (mmetab[:, :], mmeta_h[:, :]), (idenb[:, :], iden_h[:, :]),
                     (propeb[:, :], prope_h[:, :]), (onecb[:, :], onec_h[:, :]),
                     (onerb[:, :], oner_h[:, :]), (bqraw[:, :], bq_h[:, :]),
                     (bkraw[:, :], bkv_h[:, :])):
        op("POOL", dma(dst, src), incs=[c_ldx.inc()])
    consts_ld = c_ldx.n
    op("POOL", dma(a_f[0:1, 0:2048], bo_h[:, :]), incs=[c_ldx.inc()])
    op("POOL", dma(tmp4[0:1, 0:512], bkvv_h[:, :]), incs=[c_ldx.inc()])
    bcast_ld = c_ldx.n

    op("PE", lambda t: t.transpose(ps_a[:, 0:16], bqraw[:, :], idenb[0:16, 0:16]),
       waits=[W(c_ldx, bcast_ld)], incs=[c_pe.inc()])
    op("PE", lambda t: t.transpose(ps_a[:, 512:520], bkraw[:, :], idenb[0:8, 0:8]),
       incs=[c_pe.inc()])
    pe_bias = c_pe.n
    op("SE", lambda s: s.activation(bq_s[:, :], ps_a[:, 0:16], AF.Identity, scale=SCALE),
       waits=[W(c_pe, pe_bias)])
    op("SE", lambda s: s.activation(bkvk_s[:, :], ps_a[:, 512:520], AF.Identity),
       incs=[c_se.inc()])
    se_bias = c_se.n
    for j in range(4):
        op("PE", lambda t, j=j: t.matmul(ps_b[:, j * 512:(j + 1) * 512], onerb[:, :],
                                         a_f[0:1, j * 512:(j + 1) * 512],
                                         start=True, stop=True),
           waits=([W(c_ldx, bcast_ld)] if j == 0 else []))
    op("PE", lambda t: t.matmul(ps_a[:, 0:512], onerb[:, :], tmp4[0:1, 0:512],
                                start=True, stop=True),
       waits=[W(c_se, se_bias)], incs=[c_pe.inc()])
    pe_bcast = c_pe.n
    op("DVE", lambda v: v.tensor_copy(bo_bc[:, :], ps_b[:, :]),
       waits=[W(c_pe, pe_bcast)])
    op("DVE", lambda v: v.tensor_copy(bkvv_bc[:, :], ps_a[:, 0:512]))
    op("DVE", lambda v: v.memset(stats[:, :], 0.0))
    op("DVE", lambda v: v.memset(k_ctx[:, :], 0.0))
    op("DVE", lambda v: v.memset(v_ctx[:, :], 0.0), incs=[c_dv.inc()])
    st_["last_a"] = W(c_dv)
    st_["last_b"] = W(c_dv)

    cast_after = {}
    cast_sems = []

    def cast_dma(key, fn):
        c = Ctr(f"cs{len(cast_sems)}", 16)
        cast_sems.append(c)
        CTRS.append(c)
        op("POOL", fn, incs=[c.inc()])
        cast_after[key] = W(c, 16)

    def cast_wg(pc):
        cast_dma(("wg2", pc), lambda g, pc=pc: g.dma_start(
            wgb[pc * 2:(pc + 1) * 2], wg_h[pc * 2:(pc + 1) * 2]))
        cast_dma(("wu2", pc), lambda g, pc=pc: g.dma_start(
            wub[pc * 2:(pc + 1) * 2], wu_h[pc * 2:(pc + 1) * 2]))
        cast_after[("wg", pc)] = cast_after[("wg2", pc)]
        cast_after[("wu", pc)] = cast_after[("wu2", pc)]

    def cast_wdq(pc):
        cast_dma(("wdq", pc), lambda g, pc=pc: g.dma_start(
            wdb[pc * 2048:(pc + 1) * 2048, :], wd_h[pc * 2048:(pc + 1) * 2048, :]))

    cast_wg(0); cast_wg(1); cast_wg(2); cast_wg(3)
    cast_wdq(0); cast_wdq(1)
    cast_wg(4); cast_wg(5); cast_wg(6); cast_wg(7)
    cast_wdq(2); cast_wdq(3)
    cast_dma("wkv", lambda g: g.dma_start(wkvb[:, :], wkv_h[:, :]))
    cast_dma("wq", lambda g: g.dma_start(wqb[:, :, :, :], wq_h[:, :, :, :]))
    cast_dma("wo", lambda g: g.dma_start(wob[:, :], wo_h[:, :]))

    ln_round = [0]
    ln_evac = [0, 0]

    def ln_emit():
        ln_evac[0] = ln_evac[1] = 0
        ln_round[0] = 1
        tr_pe = st_["tr_pe"]
        for sub in range(4):
            xa = x_res[:, sub * 2048:(sub + 1) * 2048]
            wx = st_["xres_wr"][sub]
            op("DVE", lambda v, xa=xa, sub=sub: v.tensor_reduce(
                stats[:, sub:sub + 1], xa, AX.X, ALU.add),
               waits=[wx, W(c_dv)])
            op("SE", lambda s, xa=xa: s.activation(h_f[:, :], xa, AF.Square),
               waits=[wx, W(c_dv)], incs=[c_se.inc()])
            op("DVE", lambda v, sub=sub: v.tensor_reduce(
                stats[:, 4 + sub:5 + sub], h_f[:, :], AX.X, ALU.add),
               waits=[W(c_se)], incs=[c_dv.inc()])
            edge("DVE")
            op("DVE", lambda v, sub=sub: v.tensor_scalar_mul(
                stats[:, 8 + sub:9 + sub], stats[:, sub:sub + 1], -1.0 / 2048.0))
            op("DVE", lambda v, sub=sub: v.tensor_scalar_mul(
                stats[:, 12 + sub:13 + sub], stats[:, 4 + sub:5 + sub], 1.0 / 2048.0))
            edge("DVE")
            op("DVE", lambda v, sub=sub: v.tensor_mul(
                stats[:, 16 + sub:17 + sub], stats[:, 8 + sub:9 + sub],
                stats[:, 8 + sub:9 + sub]))
            edge("DVE")
            op("DVE", lambda v, sub=sub: v.tensor_scalar(
                stats[:, 20 + sub:21 + sub], stats[:, 12 + sub:13 + sub],
                stats[:, 16 + sub:17 + sub], EPS, ALU.subtract, ALU.add),
               incs=[c_dv.inc()])
            op("SE", lambda s, sub=sub: s.activation(
                stats[:, 24 + sub:25 + sub], stats[:, 20 + sub:21 + sub], AF.Ln),
               waits=[W(c_dv)])
            edge("SE")
            op("SE", lambda s, sub=sub: s.activation(
                stats[:, 28 + sub:29 + sub], stats[:, 24 + sub:25 + sub], AF.Exp,
                scale=-0.5),
               incs=[c_se.inc()])
            op("DVE", lambda v, sub=sub: v.tensor_mul(
                stats[:, 8 + sub:9 + sub], stats[:, 8 + sub:9 + sub],
                stats[:, 28 + sub:29 + sub]),
               waits=[W(c_se)], incs=[c_dv.inc()])
            nb_dv = c_dv.n
            nbuf = tmp4 if sub % 2 == 0 else a_f
            wts = [W(c_dv, nb_dv)]
            if tr_pe[sub % 2]:
                wts.append(W(c_pe, tr_pe[sub % 2]))
            op("SE", lambda s, xa=xa, nb=nbuf, sub=sub: s.activation(
                nb[:, :], xa, AF.Identity, bias=stats[:, 8 + sub:9 + sub],
                scale=stats[:, 28 + sub:29 + sub]),
               waits=wts, incs=[c_se.inc()])
            se_norm = c_se.n
            for rnd in range(4):
                gr = ln_round[0]
                ln_round[0] += 1
                ps = ps_b if gr % 2 == 0 else ps_a
                for j in range(4):
                    e = rnd * 4 + j
                    dst = ps[:, j * 512:j * 512 + 128]
                    srcb = nbuf[:, e * 128:(e + 1) * 128]
                    w = []
                    if j == 0:
                        w = [W(c_se, se_norm)]
                        if ln_evac[gr % 2]:
                            w.append(W(c_dv, ln_evac[gr % 2]))
                        else:
                            w.append(st_["last_b"] if gr % 2 == 0 else st_["last_a"])
                    op("PE", lambda t, d=dst, s=srcb: t.transpose(d, s, idenb[:, :]),
                       waits=w, incs=([c_pe.inc()] if j == 3 else []))
                src3 = ps[:, :].rearrange("p (b c) -> p b c", c=512)[:, :, 0:128]
                dst3 = xn_fm[:, :].rearrange("p (e c) -> p e c", c=512)[
                    :, rnd * 4:(rnd + 1) * 4, sub * 128:(sub + 1) * 128]
                op("DVE", lambda v, s3=src3, d3=dst3: v.tensor_copy(d3, s3),
                   waits=[W(c_pe)], incs=[c_dv.inc()])
                ln_evac[gr % 2] = c_dv.n
            tr_pe[sub % 2] = c_pe.n
        st_["last_b"] = W(c_dv, ln_evac[0])
        st_["last_a"] = W(c_dv, ln_evac[1])
        st_["xn_ready"] = W(c_dv)

    hs_ready = [0] * 8

    def ph1(half):
        for ql in range(8):
            q = half * 8 + ql
            base_srl = wg_serial[0]
            wff_at = {}
            for hh in range(4):
                srl = base_srl + hh
                slot = srl % 4
                dg = wg_ring[:, slot * 2048:(slot + 1) * 2048].rearrange(
                    "p (h c) -> p h c", c=512)
                du = wu_ring[:, slot * 2048:(slot + 1) * 2048].rearrange(
                    "p (h c) -> p h c", c=512)
                wg_w = r_wg.load("SP", slot,
                                 lambda g, d=dg, s=wgb[q, hh]: g.dma_start(d, s),
                                 extra_waits=[cast_after[("wg", q // 2)]])
                wu_w = r_wu.load("SE", slot,
                                 lambda g, d=du, s=wub[q, hh]: g.dma_start(d, s),
                                 extra_waits=[cast_after[("wu", q // 2)]])
                wff_at[hh] = (wg_w, wu_w)
            wg_serial[0] += 4
            for h in range(16):
                hh, h4 = h // 4, h % 4
                slot = (base_srl + hh) % 4
                for m in range(4):
                    off = slot * 2048 + h4 * 512 + m * 128
                    lg = wg_ring[:, off:off + 128]
                    rhs = xn_fm[:, h * 512:(h + 1) * 512]
                    og = ps_a[:, m * 512:(m + 1) * 512]
                    wts = []
                    if h == 0 and m == 0:
                        wts = [st_["last_a"], st_["xn_ready"]]
                    if m == 0 and h4 == 0:
                        wts.append(wff_at[hh][0])
                    incs = [c_pe.inc()] if (h == 15 and m == 3) else []
                    op("PE", lambda t, o=og, l=lg, r=rhs, st=(h == 0), sp=(h == 15):
                       t.matmul(o, l, r, start=st, stop=sp), waits=wts, incs=incs)
            pe_g_done = c_pe.n
            op("SE", lambda s: s.activation(h_f[:, :], ps_a[:, :], AF.Silu),
               waits=[W(c_pe, pe_g_done), W(c_dv)], incs=[c_se.inc()])
            st_["last_a"] = W(c_se)
            for h in range(16):
                hh, h4 = h // 4, h % 4
                slot = (base_srl + hh) % 4
                for m in range(4):
                    off = slot * 2048 + h4 * 512 + m * 128
                    lu = wu_ring[:, off:off + 128]
                    rhs = xn_fm[:, h * 512:(h + 1) * 512]
                    ou = ps_b[:, m * 512:(m + 1) * 512]
                    wts = []
                    if h == 0 and m == 0:
                        wts = [st_["last_b"], st_["xn_ready"]]
                    if m == 0 and h4 == 0:
                        wts.append(wff_at[hh][1])
                    incs = []
                    if h4 == 3 and m == 3:
                        incs = [c_pe.inc()]
                    op("PE", lambda t, o=ou, l=lu, r=rhs, st=(h == 0), sp=(h == 15):
                       t.matmul(o, l, r, start=st, stop=sp), waits=wts, incs=incs)
                    if h4 == 3 and m == 3:
                        r_wg.free_at[slot] = c_pe.n
                        r_wu.free_at[slot] = c_pe.n
            pe_done = c_pe.n
            op("DVE", lambda v: v.tensor_mul(h_f[:, :], h_f[:, :], ps_b[:, :]),
               waits=[W(c_pe, pe_done), W(c_se)], incs=[c_dv.inc()])
            st_["last_b"] = W(c_dv)
            edge("DVE")
            op("DVE", lambda v: v.tensor_single_scalar(
                a_f[:, :].bitcast(mybir.dt.int32), h_f[:, :].bitcast(mybir.dt.int32),
                0x7FFFFFFF, ALU.bitwise_and))
            sl = [a_f[:, i * 512:(i + 1) * 512] for i in range(4)]
            tm = [tmp4[:, i * 512:(i + 1) * 512] for i in range(4)]
            mb = [mskf[:, i * 512:(i + 1) * 512] for i in range(3)]
            edge("DVE")
            op("DVE", lambda v: v.tensor_max(tm[0], sl[0], sl[1]))
            op("DVE", lambda v: v.tensor_max(tm[1], sl[2], sl[3]))
            op("DVE", lambda v: v.tensor_tensor(tm[2], sl[0], sl[1], ALU.min))
            op("DVE", lambda v: v.tensor_tensor(tm[3], sl[2], sl[3], ALU.min))
            edge("DVE")
            op("DVE", lambda v: v.tensor_tensor(tsel[:, :], tm[0], tm[1], ALU.min))
            op("DVE", lambda v: v.tensor_max(mb[2], tm[2], tm[3]))
            edge("DVE")
            op("DVE", lambda v: v.tensor_max(tsel[:, :], tsel[:, :], mb[2]))
            edge("DVE")
            bufs = [0, 1, 2, 0]

            def cmp_m(m):
                op("DVE", lambda v, m=m: v.tensor_tensor(
                    mb[bufs[m]], sl[m], tsel[:, :], ALU.is_ge))

            def mul_m(m, inc=False):
                c_loc = ql * 4 + m
                dsth = hs[:, c_loc * 512:(c_loc + 1) * 512]
                op("DVE", lambda v, m=m, d=dsth: v.tensor_mul(
                    d, h_f[:, m * 512:(m + 1) * 512], mb[bufs[m]]),
                   incs=([c_dv.inc()] if inc else []))

            cmp_m(0); cmp_m(1); cmp_m(2)
            edge("DVE")
            mul_m(0); mul_m(1)
            edge("DVE")
            cmp_m(3); mul_m(2)
            edge("DVE")
            mul_m(3, inc=True)
            hs_ready[ql] = c_dv.n

    def ph2(half):
        for pair in range(2):
            subs = (0, 1) if pair == 0 else (2, 3)
            yps = {subs[0]: ps_a, subs[1]: ps_b}
            for i in range(32):
                rowbase = half * 4096 + i * 128
                slot = wd_serial[0] % 4
                wd_serial[0] += 1
                dst = wd_ring[:, slot * 2048:(slot + 1) * 2048]
                wd_at = r_wd.load(
                    "SE" if slot % 2 == 0 else "SP", slot,
                    lambda s, d=dst, sc=wdb[rowbase:rowbase + 128, :]: s.dma_start(d, sc),
                    extra_waits=[cast_after[("wdq", half * 2 + i // 16)]])
                for sub in subs:
                    lhs = hs[:, i * 512 + sub * 128:i * 512 + sub * 128 + 128]
                    for ocol in range(4):
                        o = yps[sub][:, ocol * 512:(ocol + 1) * 512]
                        r = wd_ring[:, slot * 2048 + ocol * 512:slot * 2048 + (ocol + 1) * 512]
                        wts2 = []
                        if i == 0 and ocol == 0:
                            wts2.append(st_["last_a"] if sub == subs[0] else st_["last_b"])
                        if ocol == 0 and sub == subs[0]:
                            wts2.append(W(c_dv, hs_ready[i // 4]))
                            wts2.append(wd_at)
                        incs = []
                        if sub == subs[1] and ocol == 3:
                            incs = [c_pe.inc()]
                        op("PE", lambda t, o=o, l=lhs, r=r, st=(i == 0), sp=(i == 31):
                           t.matmul(o, l, r, start=st, stop=sp), waits=wts2, incs=incs)
                        if sub == subs[1] and ocol == 3:
                            r_wd.free_at[slot] = c_pe.n
            pe_y = c_pe.n
            for sub in subs:
                xa = x_res[:, sub * 2048:(sub + 1) * 2048]
                op("DVE", lambda v, xa=xa, p=yps[sub]: v.scalar_tensor_tensor(
                    xa, p[:, :], 0.5, xa, ALU.mult, ALU.add),
                   waits=[W(c_pe, pe_y)], incs=[c_dv.inc()])
                st_["xres_wr"][sub] = W(c_dv)
                if sub == subs[0]:
                    st_["last_a"] = W(c_dv)
                else:
                    st_["last_b"] = W(c_dv)

    def attn_emit(t_i):
        for h in range(16):
            slot = h % 4
            dst = wkv_ring[:, slot * 1024:(slot + 1) * 1024]
            wat_w = r_wkv.load(
                "SP", slot,
                lambda g, d=dst, s=wkvb[h * 128:(h + 1) * 128, :]: g.dma_start(d, s),
                extra_waits=[cast_after["wkv"]])
            for g_ in range(4):
                lk = wkv_ring[:, slot * 1024 + g_ * 128:slot * 1024 + (g_ + 1) * 128]
                rhs = xn_fm[:, h * 512:(h + 1) * 512]
                o = ps_a[:, g_ * 512:(g_ + 1) * 512]
                wts2 = []
                if g_ == 0:
                    wts2.append(wat_w)
                    if h == 0:
                        wts2 += [st_["last_a"], st_["last_b"]]
                op("PE", lambda t, o=o, l=lk, r=rhs, st=(h == 0), sp=(h == 15):
                   t.matmul(o, l, r, start=st, stop=sp), waits=wts2)
            for sub in range(4):
                lx = xn_fm[:, h * 512 + sub * 128:h * 512 + sub * 128 + 128]
                rv = wkv_ring[:, slot * 1024 + 512:slot * 1024 + 1024]
                o = ps_b[:, sub * 512:(sub + 1) * 512]
                op("PE", lambda t, o=o, l=lx, r=rv, st=(h == 0), sp=(h == 15):
                   t.matmul(o, l, r, start=st, stop=sp),
                   incs=([c_pe.inc()] if sub == 3 else []))
                if sub == 3:
                    r_wkv.free_at[slot] = c_pe.n
        pe_kv = c_pe.n
        for sub in range(4):
            dst = v_ctx[:, (5 + sub) * 512:(6 + sub) * 512]
            o = ps_b[:, sub * 512:(sub + 1) * 512]
            op("DVE", lambda v, d=dst, o=o: v.tensor_add(d, o, bkvv_bc[:, :]),
               waits=([W(c_pe, pe_kv)] if sub == 0 else []),
               incs=([c_dv.inc()] if sub == 3 else []))
        st_["last_b"] = W(c_dv)
        prev_rot_read = W(c_dv, 0)
        for g_ in range(4):
            kps = ps_a[:, g_ * 512:(g_ + 1) * 512]
            wts = [W(c_pe, pe_kv)]
            if g_ > 0:
                wts.append(prev_rot_read)
                wts.append(W(c_pe))
            op("SE", lambda s, o=kps, g_=g_: s.activation(
                qb_f[:, 0:512], o, AF.Identity, bias=bkvk_s[:, g_:g_ + 1]),
               waits=wts, incs=[c_se.inc()])
            se_kb = c_se.n
            op("PE", lambda t: t.matmul(ps_b[:, 0:512], propeb[:, :], qb_f[:, 0:512],
                                        start=True, stop=True),
               waits=[W(c_se, se_kb), st_["last_b"], W(c_ldx, st_["cos_ready"])],
               incs=[c_pe.inc()])
            pe_rot = c_pe.n
            op("DVE", lambda v: v.tensor_mul(rt1[:, :], qb_f[:, 0:512], cos_t[:, :]),
               waits=[W(c_pe, pe_rot)])
            op("DVE", lambda v: v.tensor_mul(rt2[:, :], ps_b[:, 0:512], sin_t[:, :]))
            edge("DVE")
            kdst = k_ctx[:, g_ * 1152 + 640:g_ * 1152 + 1152]
            op("DVE", lambda v, d=kdst: v.tensor_add(d, rt1[:, :], rt2[:, :]),
               incs=[c_dv.inc()])
            prev_rot_read = W(c_dv)
            st_["last_b"] = W(c_dv)
        st_["last_a"] = W(c_dv)
        dv_kctx = c_dv.n

        qbuf_dv = [0, 0]
        qbank_se = [0, 0, 0, 0]
        for e in range(16):
            slot = e % 2
            dq = wq_ring[:, slot * 2048:(slot + 1) * 2048].rearrange(
                "p (h c) -> p h c", c=128)
            wat_w = r_wq.load("SP", slot,
                              lambda g, d=dq, s=wqb[e]: g.dma_start(d, s),
                              extra_waits=[cast_after["wq"]])
            bank = e % 4
            o = ps_a[:, bank * 512:(bank + 1) * 512]
            for h in range(16):
                l = wq_ring[:, slot * 2048 + h * 128:slot * 2048 + (h + 1) * 128]
                rhs = xn_fm[:, h * 512:(h + 1) * 512]
                wts2 = []
                if h == 0:
                    wts2.append(wat_w)
                    if e == 0:
                        wts2.append(st_["last_a"])
                    if qbank_se[bank]:
                        wts2.append(W(c_se, qbank_se[bank]))
                op("PE", lambda t, o=o, l=l, r=rhs, st=(h == 0), sp=(h == 15):
                   t.matmul(o, l, r, start=st, stop=sp), waits=wts2,
                   incs=([c_pe.inc()] if h == 15 else []))
                if h == 15:
                    r_wq.free_at[slot] = c_pe.n
            pe_q = c_pe.n
            qb = qb_f[:, (e % 2) * 512:(e % 2) * 512 + 512]
            wts = [W(c_pe, pe_q)]
            if qbuf_dv[e % 2]:
                wts.append(W(c_dv, qbuf_dv[e % 2]))
            op("SE", lambda s, o=o, qb=qb, e=e: s.activation(
                qb, o, AF.Identity, bias=bq_s[:, e:e + 1], scale=SCALE),
               waits=wts, incs=[c_se.inc()])
            qbank_se[bank] = c_se.n
            op("PE", lambda t, qb=qb: t.matmul(ps_b[:, 0:512], propeb[:, :], qb,
                                               start=True, stop=True),
               waits=[W(c_se), W(c_dv)], incs=[c_pe.inc()])
            pe_rot = c_pe.n
            op("DVE", lambda v, qb=qb: v.tensor_mul(rt1[:, :], qb, cos_t[:, :]),
               waits=[W(c_pe, pe_rot)])
            op("DVE", lambda v: v.tensor_mul(rt2[:, :], ps_b[:, 0:512], sin_t[:, :]))
            edge("DVE")
            qdst = q_fm[:, e * 512:(e + 1) * 512]
            op("DVE", lambda v, d=qdst: v.tensor_add(d, rt1[:, :], rt2[:, :]),
               incs=[c_dv.inc()])
            qbuf_dv[e % 2] = c_dv.n
        dv_q = c_dv.n
        se_qb_final = c_se.n
        st_["rope_done"] = W(c_dv, dv_q)

        if t_i == 0:
            blocks = list(range(5, 9))
        elif t_i == 1:
            blocks = list(range(1, 9))
        else:
            blocks = list(range(0, 9))

        prev_k2_pe = 0
        for g2 in range(4):
            wts = [W(c_dv, dv_kctx)]
            if prev_k2_pe:
                wts.append(W(c_pe, prev_k2_pe))
            op("SE", lambda s, g2=g2: s.activation(
                k2b[:, :], k_ctx[:, g2 * 1152:(g2 + 1) * 1152], AF.Square),
               waits=wts, incs=[c_se.inc()])
            se_k2 = c_se.n
            spans = [(0, 512, ps_b[0:1, 1024:1536]),
                     (512, 1024, ps_b[0:1, 1536:2048]),
                     (1024, 1152, ps_b[0:1, 512:640])]
            for si, (c0, c1, o) in enumerate(spans):
                wts2 = [W(c_se, se_k2)]
                if si == 0:
                    wts2.append(W(c_dv))
                    if g2 == 0:
                        wts2.append(st_["last_b"])
                op("PE", lambda t, o=o, c0=c0, c1=c1: t.matmul(
                    o, onecb[:, :], k2b[:, c0:c1], start=True, stop=True),
                   waits=wts2, incs=([c_pe.inc()] if si == 2 else []))
            prev_k2_pe = c_pe.n
            op("DVE", lambda v: v.tensor_reduce(srow[0:1, 4:5], ps_b[0:1, 1024:1536],
                                                AX.X, ALU.max),
               waits=[W(c_pe, prev_k2_pe)])
            op("DVE", lambda v: v.tensor_reduce(srow[0:1, 5:6], ps_b[0:1, 1536:2048],
                                                AX.X, ALU.max))
            op("DVE", lambda v: v.tensor_reduce(srow[0:1, 6:7], ps_b[0:1, 512:640],
                                                AX.X, ALU.max))
            edge("DVE")
            op("DVE", lambda v: v.tensor_max(srow[0:1, 4:5], srow[0:1, 4:5],
                                             srow[0:1, 5:6]))
            edge("DVE")
            op("DVE", lambda v, g2=g2: v.tensor_max(
                srow[0:1, g2:g2 + 1], srow[0:1, 4:5], srow[0:1, 6:7]),
               incs=[c_dv.inc()])
        c2_dv = c_dv.n
        st_["last_b"] = W(c_dv)

        nm_dv = {}
        sqrt_se = {}
        nmcopy_dv = {}
        lbank = {}

        def stats_emit(hd):
            g2 = hd // 4
            pp = hd % 2
            qsl2 = q_fm[:, hd * 512:(hd + 1) * 512]
            op("SE", lambda s, q=qsl2: s.activation(q2b[:, :], q, AF.Square),
               waits=[W(c_dv, dv_q)], incs=[c_se.inc()])
            se_q2 = c_se.n
            lb = ps_b[0:1, 1024:1536] if hd % 2 == 0 else ps_b[0:1, 0:512]
            lbank[hd] = lb
            wts = [W(c_se, se_q2)]
            if hd >= 2 and (hd - 2) in sqrt_se:
                wts.append(W(c_dv, recip_dv[hd - 2]))
            elif hd >= 1:
                wts.append(W(c_se, sqrt_se[hd - 1]))
            else:
                wts.append(W(c_dv, c2_dv))
            op("PE", lambda t, lb=lb: t.matmul(lb, onecb[:, :], q2b[:, :],
                                               start=True, stop=True),
               waits=wts, incs=[c_pe.inc()])
            pe_q2 = c_pe.n
            ar = arow[0:1, pp * 512:pp * 512 + 512]
            op("SE", lambda s, g2=g2, ar=ar, lb=lb: s.activation(
                ar, lb, AF.Sqrt, scale=srow[0:1, g2:g2 + 1]),
               waits=[W(c_pe, pe_q2), W(c_dv, c2_dv)], incs=[c_se.inc()])
            sqrt_se[hd] = c_se.n
            nr = nmrow[0:1, pp:pp + 1]
            op("DVE", lambda v, ar=ar, nr=nr: v.tensor_reduce(
                nr, ar, AX.X, ALU.max, negate=True),
               waits=[W(c_se)], incs=[c_dv.inc()])
            dv_nr = c_dv.n
            wts = [W(c_dv, dv_nr)]
            if (hd - 1) in nmcopy_dv:
                wts.append(W(c_dv, nmcopy_dv[hd - 1]))
            op("PE", lambda t, nr=nr: t.matmul(ps_b[:, 1536:1537], onerb[:, :], nr,
                                               start=True, stop=True),
               waits=wts, incs=[c_pe.inc()])
            pe_nm = c_pe.n
            op("DVE", lambda v, pp=pp: v.tensor_copy(nm128[:, pp:pp + 1],
                                                     ps_b[:, 1536:1537]),
               waits=[W(c_pe, pe_nm)], incs=[c_dv.inc()])
            nm_dv[hd] = c_dv.n
            nmcopy_dv[hd] = c_dv.n

        stats_emit(0)

        nb = len(blocks)
        triples = [blocks[i:i + 3] for i in range(0, nb, 3)]
        ntr = len(triples)
        exp3_done = {}
        mask_dv = {}
        recip_dv = {}
        ymul_dv = {}

        def emit_triple(hd, ti):
            g2 = hd // 4
            pp = hd % 2
            qsl2 = q_fm[:, hd * 512:(hd + 1) * 512]
            trp = triples[ti]
            for j, b in enumerate(trp):
                o = ps_a[:, j * 512:(j + 1) * 512]
                lk = k_ctx[:, g2 * 1152 + b * 128:g2 * 1152 + (b + 1) * 128]
                wts2 = []
                if j == 0:
                    if ti == 0 and hd == 0:
                        wts2 = [W(c_se, se_qb_final), W(c_dv, dv_q), st_["last_a"]]
                    else:
                        prev = exp3_done.get((hd, ti - 1)) or                             exp3_done.get((hd - 1, ntr - 1))
                        if prev:
                            wts2 = [W(c_se, prev)]
                op("PE", lambda t, o=o, l=lk, q=qsl2: t.matmul(
                    o, l, q, start=True, stop=True),
                   waits=wts2, incs=([c_pe.inc()] if j == len(trp) - 1 else []))
            pe_s = c_pe.n
            b0 = trp[0]
            w = len(trp) * 512
            pdst = p_sb[:, b0 * 512:b0 * 512 + w]
            op("SE", lambda s, d=pdst, w=w, pp=pp: s.activation(
                d, ps_a[:, 0:w], AF.Exp, bias=nm128[:, pp:pp + 1]),
               waits=[W(c_pe, pe_s), W(c_dv, nm_dv[hd])], incs=[c_se.inc()])
            exp3_done[(hd, ti)] = c_se.n
            for b in trp:
                pd = p_sb[:, b * 512:(b + 1) * 512]
                if b == 0:
                    msl = mmetab[:, :]
                elif t_i == 1 and b == 1:
                    msl = mb1b[:, :]
                else:
                    msl = stripb[:, 1024 - 128 * b:1536 - 128 * b]
                op("DVE", lambda v, d=pd, m=msl: v.tensor_mul(d, d, m),
                   waits=[W(c_se)], incs=[c_dv.inc()])
                mask_dv[(hd, b)] = c_dv.n

        def emit_lav(hd, ti):
            g2 = hd // 4
            base = sum(len(triples[t]) for t in range(ti))
            for k, b in enumerate(triples[ti]):
                bi = base + k
                psl = p_sb[:, b * 512:(b + 1) * 512]
                first, last = (bi == 0), (bi == nb - 1)
                wts2 = [W(c_dv, mask_dv[(hd, b)])]
                if first:
                    wts2.append(W(c_se, sqrt_se[hd]))
                op("PE", lambda t, p=psl, st=first, sp=last, lb=lbank[hd]: t.matmul(
                    lb, onecb[:, :], p, start=st, stop=sp),
                   waits=wts2)
                lv = v_ctx[:, b * 512 + g2 * 128:b * 512 + (g2 + 1) * 128]
                wts3 = []
                if first and hd > 0:
                    wts3.append(W(c_dv, ymul_dv[hd - 1]))
                op("PE", lambda t, l=lv, p=psl, st=first, sp=last: t.matmul(
                    ps_a[:, 1536:2048], l, p, start=st, stop=sp),
                   waits=wts3, incs=([c_pe.inc()] if last else []))

        def finalize(hd):
            pe_lav = c_pe.n
            op("DVE", lambda v, lb=lbank[hd]: v.reciprocal(linv[:, :], lb),
               waits=[W(c_pe, pe_lav)], incs=[c_dv.inc()])
            recip_dv[hd] = c_dv.n
            op("PE", lambda t: t.matmul(ps_b[:, 512:1024], onerb[:, :], linv[:, :],
                                        start=True, stop=True),
               waits=[W(c_dv)], incs=[c_pe.inc()])
            pe_rb = c_pe.n
            op("DVE", lambda v: v.tensor_copy(recip_bc[:, :], ps_b[:, 512:1024]),
               waits=[W(c_pe, pe_rb)])
            edge("DVE")
            ydst = q_fm[:, hd * 512:(hd + 1) * 512]
            op("DVE", lambda v, d=ydst: v.tensor_mul(d, ps_a[:, 1536:2048],
                                                     recip_bc[:, :]),
               incs=[c_dv.inc()])
            ymul_dv[hd] = c_dv.n
            st_["last_a"] = W(c_dv)

        for hd_i in range(16):
            for ti in range(ntr):
                emit_triple(hd_i, ti)
                if ti == ntr - 1 and hd_i + 1 < 16:
                    stats_emit(hd_i + 1)
                if ti >= 1:
                    emit_lav(hd_i, ti - 1)
                elif hd_i > 0:
                    emit_lav(hd_i - 1, ntr - 1)
                    finalize(hd_i - 1)
        emit_lav(15, ntr - 1)
        finalize(15)
        dv_y = c_dv.n

        pe_done = c_pe.n
        for g_ in range(4):
            src = k_ctx[:, g_ * 1152 + 640:g_ * 1152 + 1152]
            dst = k_ctx[:, g_ * 1152 + 128:g_ * 1152 + 640]
            op("DVE", lambda v, d=dst, s=src: v.tensor_copy(d, s),
               waits=([W(c_pe, pe_done)] if g_ == 0 else []))
        for b in range(4):
            src = v_ctx[:, (5 + b) * 512:(6 + b) * 512]
            dst = v_ctx[:, (1 + b) * 512:(2 + b) * 512]
            op("DVE", lambda v, d=dst, s=src: v.tensor_copy(d, s))
        if t_i == 0:
            for g_ in range(4):
                src = k_ctx[:, g_ * 1152 + 640:g_ * 1152 + 656]
                dst = k_ctx[:, g_ * 1152:g_ * 1152 + 16]
                op("DVE", lambda v, d=dst, s=src: v.tensor_copy(d, s))
            op("DVE", lambda v: v.tensor_copy(v_ctx[0:16, 0:512],
                                              v_ctx[0:16, 5 * 512:6 * 512]))
        op("DVE", lambda v: v.tensor_copy(stats[:, 32:33], stats[:, 33:34]),
           incs=[c_dv.inc()])

        for pairi in range(2):
            subs = (0, 1) if pairi == 0 else (2, 3)
            yps = {subs[0]: ps_a, subs[1]: ps_b}
            for e in range(16):
                slot = e % 2
                dst = wo_ring[:, slot * 2048:(slot + 1) * 2048]
                wat_w = r_wo.load(
                    "SP", slot,
                    lambda g, d=dst, s=wob[e * 128:(e + 1) * 128, :]: g.dma_start(d, s),
                    extra_waits=[cast_after["wo"]])
                for sub in subs:
                    lhs = q_fm[:, e * 512 + sub * 128:e * 512 + sub * 128 + 128]
                    for ocol in range(4):
                        o = yps[sub][:, ocol * 512:(ocol + 1) * 512]
                        r = wo_ring[:, slot * 2048 + ocol * 512:slot * 2048 + (ocol + 1) * 512]
                        wts2 = []
                        if e == 0 and ocol == 0:
                            wts2.append(st_["last_a"] if sub == subs[0] else st_["last_b"])
                            wts2.append(W(c_dv, dv_y))
                        if ocol == 0 and sub == subs[0]:
                            wts2.append(wat_w)
                        incs = []
                        if sub == subs[1] and ocol == 3:
                            incs = [c_pe.inc()]
                        op("PE", lambda t, o=o, l=lhs, r=r, st=(e == 0), sp=(e == 15):
                           t.matmul(o, l, r, start=st, stop=sp), waits=wts2, incs=incs)
                        if sub == subs[1] and ocol == 3:
                            r_wo.free_at[slot] = c_pe.n
            pe_o = c_pe.n
            for sub in subs:
                xa = x_res[:, sub * 2048:(sub + 1) * 2048]
                p = yps[sub]
                op("DVE", lambda v, xa=xa, p=p: v.tensor_add(xa, xa, p[:, :]),
                   waits=[W(c_pe, pe_o)])
                edge("DVE")
                op("DVE", lambda v, xa=xa: v.tensor_add(xa, xa, bo_bc[:, :]),
                   incs=[c_dv.inc()])
                st_["xres_wr"][sub] = W(c_dv)
                if sub == subs[0]:
                    st_["last_a"] = W(c_dv)
                else:
                    st_["last_b"] = W(c_dv)


    for t_i in range(n_tiles):
        for sub in range(4):
            dst = x_res[:, sub * 2048:(sub + 1) * 2048]
            src = x_h[t_i * 512 + sub * 128:t_i * 512 + (sub + 1) * 128, :]
            wts = [W(c_outs[sub], st_["out_done"][sub])] if t_i > 0 else (
                [W(c_ldx)] if sub == 0 else [])
            op("POOL", lambda g, d=dst, s=src: g.dma_start(d, s),
               waits=wts, incs=[c_xins[sub].inc()])
        wts = [st_["rope_done"]] if t_i > 0 else []
        op("POOL", lambda g, t=t_i: g.dma_start(cos_t[:, :],
                                                cos_h[:, t * 512:(t + 1) * 512]),
           waits=wts, incs=[c_ldx.inc()])
        op("POOL", lambda g, t=t_i: g.dma_start(sin_t[:, :],
                                                sin_h[:, t * 512:(t + 1) * 512]),
           incs=[c_ldx.inc()])
        st_["cos_ready"] = c_ldx.n
        for sub in range(4):
            st_["xres_wr"][sub] = W(c_xins[sub], c_xins[sub].n)

        ln_emit()
        ph1(0); ph2(0); ph1(1); ph2(1)
        r_wkv.bump_all(c_pe.n); r_wq.bump_all(c_pe.n); r_wo.bump_all(c_pe.n)
        ln_emit()
        attn_emit(t_i)
        r_wg.bump_all(c_pe.n); r_wu.bump_all(c_pe.n)
        ln_emit()
        ph1(0); ph2(0); ph1(1); ph2(1)
        r_wkv.bump_all(c_pe.n); r_wq.bump_all(c_pe.n); r_wo.bump_all(c_pe.n)
        for sub in range(4):
            src = x_res[:, sub * 2048:(sub + 1) * 2048]
            dst = out_h[t_i * 512 + sub * 128:t_i * 512 + (sub + 1) * 128, :]
            op("POOL", lambda g, d=dst, s=src: g.dma_start(d, s),
               waits=[st_["xres_wr"][sub]], incs=[c_outs[sub].inc()])
            st_["out_done"][sub] = c_outs[sub].n

    final_outs = [c.n for c in c_outs]

    with contextlib.ExitStack() as stk:
        for c in CTRS:
            if c.name != "wat_dummy":
                c.h = stk.enter_context(nc.semaphore(c.name))
        blk = stk.enter_context(nc.Block())

        def replay(name, eng):
            hw = {}
            for waits, fn, incs in plan[name]:
                for (ctr, val) in waits:
                    if val <= 0 or hw.get(ctr.name, 0) >= val:
                        continue
                    eng.wait_ge(ctr.h, val)
                    hw[ctr.name] = val
                instr = fn(eng)
                for (ctr, amt) in incs:
                    instr = instr.then_inc(ctr.h, amt)

        @blk.tensor
        def _(t):
            replay("PE", t)

        @blk.vector
        def _(v):
            replay("DVE", v)

        @blk.scalar
        def _(s):
            replay("SE", s)

        @blk.sync
        def _(s):
            replay("SP", s)

        @blk.gpsimd
        def _(g):
            replay("POOL", g)
            for c, v in zip(c_outs, final_outs):
                if v:
                    g.wait_ge(c.h, v)

    return nc


def _perm():
    c = np.arange(II)
    return 512 * (c // 512) + 4 * (c % 128) + (c % 512) // 128


def _host_constants():
    import ml_dtypes
    inv = 1.0 / 10000.0 ** (np.arange(0, HD, 2, dtype=np.float64) / HD)
    ang = np.arange(S, dtype=np.float64)[None, :] * inv[:, None]
    cosfm = np.ascontiguousarray(np.repeat(np.cos(ang), 2, axis=0).astype(np.float32))
    sinfm = np.ascontiguousarray(np.repeat(np.sin(ang), 2, axis=0).astype(np.float32))
    k = np.arange(128)[:, None]
    u = np.arange(1408)[None, :]
    band = ((u - 384 - k >= 0) & (u - 384 - k <= 511)).astype(np.float32)
    strip = band.astype(ml_dtypes.bfloat16)
    b1 = band[:, 896:1408].copy()
    b1[k[:, 0] < 16, :] = 1.0
    maskb1 = b1.astype(ml_dtypes.bfloat16)
    maskmeta = np.zeros((128, 512), np.float32)
    maskmeta[:16, :] = 1.0
    maskmeta = maskmeta.astype(ml_dtypes.bfloat16)
    iden = np.eye(128, dtype=np.float32)
    pr = np.zeros((128, 128), np.float32)
    for i in range(64):
        pr[2 * i + 1, 2 * i] = -1.0
        pr[2 * i, 2 * i + 1] = 1.0
    onec = np.ones((128, 1), ml_dtypes.bfloat16)
    oner = np.ones((1, 128), np.float32)
    return dict(cosfm=cosfm, sinfm=sinfm, strip=strip, maskb1=maskb1,
                maskmeta=maskmeta, idenf=iden, prope=pr, onec_bf=onec,
                oner_f=oner)


def _tile_weights(Wq, w_gate, w_up, w_down):
    perm = _perm()
    wg_p = w_gate[:, perm]
    wu_p = w_up[:, perm]
    wd_p = np.ascontiguousarray(w_down[perm, :])
    a = wg_p.reshape(4, 4, 128, 16, 512)
    wg_t = np.ascontiguousarray(np.transpose(a, (3, 0, 2, 1, 4)))
    a = wu_p.reshape(4, 4, 128, 16, 512)
    wu_t = np.ascontiguousarray(np.transpose(a, (3, 0, 2, 1, 4)))
    a = Wq.reshape(16, 128, 16, 128)
    wq_t = np.ascontiguousarray(np.transpose(a, (2, 1, 0, 3)))
    return wg_t, wu_t, wd_p, wq_t


def make_inputs_common(Wq, bq, Wkv, bkv, Wo, bo, wg, wu, wd):
    wg_t, wu_t, wd_p, wq_t = _tile_weights(Wq, wg, wu, wd)
    consts = _host_constants()
    return {
        "wg_t": wg_t, "wu_t": wu_t, "wd_t": wd_p, "wq_t": wq_t,
        "wkv_t": np.ascontiguousarray(Wkv), "wo_t": np.ascontiguousarray(Wo),
        "bq": np.ascontiguousarray(bq.reshape(16, 128)),
        "bkv": np.ascontiguousarray(bkv.reshape(8, 128)),
        "bkvv": np.ascontiguousarray(bkv[512:].reshape(1, 512)),
        "bo_r": np.ascontiguousarray(bo.reshape(1, E)),
        **consts,
    }


def kernel(L0_attn_x, L0_attn_Wq, L0_attn_bq, L0_attn_Wkv, L0_attn_bkv,
           L0_attn_Wo, L0_attn_bo, L0_ffn_w_gate, L0_ffn_w_up, L0_ffn_w_down):
    x = np.asarray(L0_attn_x, dtype=np.float32)
    common = make_inputs_common(
        np.asarray(L0_attn_Wq, np.float32), np.asarray(L0_attn_bq, np.float32),
        np.asarray(L0_attn_Wkv, np.float32), np.asarray(L0_attn_bkv, np.float32),
        np.asarray(L0_attn_Wo, np.float32), np.asarray(L0_attn_bo, np.float32),
        np.asarray(L0_ffn_w_gate, np.float32), np.asarray(L0_ffn_w_up, np.float32),
        np.asarray(L0_ffn_w_down, np.float32))
    nc = build_nc(n_tiles=8)
    in_maps = [dict(common, x=np.ascontiguousarray(x[b])) for b in range(8)]
    res = run_bass_kernel_spmd(nc, in_maps, core_ids=list(range(8)))
    return np.stack([res.results[b]["out"] for b in range(8)], axis=0)


# version 16 (final)
# speedup vs baseline: 1.1325x; 1.0320x over previous
import numpy as np
import contextlib
import concourse.bass as bass
import concourse.mybir as mybir
from concourse.bass_utils import run_bass_kernel_spmd

F32 = mybir.dt.float32
BF16 = mybir.dt.bfloat16
AF = mybir.ActivationFunctionType
ALU = mybir.AluOpType
AX = mybir.AxisListType

S, E, T, HD, NH, NKV, II = 4096, 2048, 512, 128, 16, 4, 8192
SCALE = float(1.0 / np.sqrt(128.0))
EPS = 1e-5
ENGINES = ("PE", "DVE", "SE", "SP", "POOL")


def build_nc(n_tiles=8):
    nc = bass.Bass("TRN2", target_bir_lowering=False, debug=False)

    x_h = nc.dram_tensor("x", [S, E], F32, kind="ExternalInput").ap()
    wg_h = nc.dram_tensor("wg_t", [16, 4, 128, 4, 512], F32, kind="ExternalInput").ap()
    wu_h = nc.dram_tensor("wu_t", [16, 4, 128, 4, 512], F32, kind="ExternalInput").ap()
    wd_h = nc.dram_tensor("wd_t", [II, E], F32, kind="ExternalInput").ap()
    wq_h = nc.dram_tensor("wq_t", [16, 128, 16, 128], F32, kind="ExternalInput").ap()
    wkv_h = nc.dram_tensor("wkv_t", [E, 1024], F32, kind="ExternalInput").ap()
    wo_h = nc.dram_tensor("wo_t", [E, E], F32, kind="ExternalInput").ap()
    bq_h = nc.dram_tensor("bq", [16, 128], F32, kind="ExternalInput").ap()
    bkv_h = nc.dram_tensor("bkv", [8, 128], F32, kind="ExternalInput").ap()
    bkvv_h = nc.dram_tensor("bkvv", [1, 512], F32, kind="ExternalInput").ap()
    bo_h = nc.dram_tensor("bo_r", [1, E], F32, kind="ExternalInput").ap()
    cos_h = nc.dram_tensor("cosfm", [128, S], F32, kind="ExternalInput").ap()
    sin_h = nc.dram_tensor("sinfm", [128, S], F32, kind="ExternalInput").ap()
    strip_h = nc.dram_tensor("strip", [128, 1408], BF16, kind="ExternalInput").ap()
    mb1_h = nc.dram_tensor("maskb1", [128, 512], BF16, kind="ExternalInput").ap()
    mmeta_h = nc.dram_tensor("maskmeta", [128, 512], BF16, kind="ExternalInput").ap()
    iden_h = nc.dram_tensor("idenf", [128, 128], F32, kind="ExternalInput").ap()
    prope_h = nc.dram_tensor("prope", [128, 128], F32, kind="ExternalInput").ap()
    onec_h = nc.dram_tensor("onec_bf", [128, 1], BF16, kind="ExternalInput").ap()
    oner_h = nc.dram_tensor("oner_f", [1, 128], F32, kind="ExternalInput").ap()
    out_h = nc.dram_tensor("out", [S, E], F32, kind="ExternalOutput").ap()

    wgb = nc.dram_tensor("wg_bf", [16, 4, 128, 4, 512], BF16, kind="Internal").ap()
    wub = nc.dram_tensor("wu_bf", [16, 4, 128, 4, 512], BF16, kind="Internal").ap()
    wdb = nc.dram_tensor("wd_bf", [II, E], BF16, kind="Internal").ap()
    wqb = nc.dram_tensor("wq_bf", [16, 128, 16, 128], BF16, kind="Internal").ap()
    wkvb = nc.dram_tensor("wkv_bf", [E, 1024], BF16, kind="Internal").ap()
    wob = nc.dram_tensor("wo_bf", [E, E], BF16, kind="Internal").ap()

    A = nc.alloc_sbuf_tensor
    x_res = A("x_res", [128, 4 * 2048], F32)
    xn_fm = A("xn_fm", [128, 16 * 512], BF16)
    k_ctx = A("k_ctx", [128, 4 * 1152], BF16)
    v_ctx = A("v_ctx", [128, 9 * 512], BF16)
    cos_t = A("cos_t", [128, 512], F32)
    sin_t = A("sin_t", [128, 512], F32)
    bo_bc = A("bo_bc", [128, 2048], F32)
    bkvv_bc = A("bkvv_bc", [128, 512], F32)
    stripb = A("stripb", [128, 1408], BF16)
    mb1b = A("mb1b", [128, 512], BF16)
    mmetab = A("mmetab", [128, 512], BF16)
    idenb = A("idenb", [128, 128], F32)
    propeb = A("propeb", [128, 128], F32)
    onecb = A("onecb", [128, 1], BF16)
    onerb = A("onerb", [1, 128], F32)
    bq_s = A("bq_s", [128, 16], F32)
    bkvk_s = A("bkvk_s", [128, 8], F32)
    bqraw = A("bqraw", [16, 128], F32)
    bkraw = A("bkraw", [8, 128], F32)
    stats = A("stats", [128, 36], F32)
    srow = A("srow", [1, 16], F32)
    arow = A("arow", [1, 1024], F32)
    nmrow = A("nmrow", [1, 2], F32)
    nm128 = A("nm128", [128, 2], F32)
    linv = A("linv", [1, 512], F32)
    h_f = A("h_f", [128, 2048], F32)
    a_f = A("a_f", [128, 2048], F32)
    tmp4 = A("tmp4", [128, 2048], F32)
    tsel = A("tsel", [128, 512], F32)
    mskf = A("mskf", [128, 3 * 512], F32)
    wd_ring = A("wd_ring", [128, 4 * 2048], BF16)
    arena = A("arena", [128, 16384], F32)
    arena_bf = arena.bitcast(BF16)
    hs = arena_bf[:, 0:16384]
    wg_ring = arena_bf[:, 16384:24576]
    wu_ring = arena_bf[:, 24576:32768]
    q_fm = arena_bf[:, 0:8192]
    p_sb = arena_bf[:, 8192:12800]
    qb_f = arena[:, 6400:7424]
    rt1 = arena[:, 7424:7936]
    rt2 = arena[:, 7936:8448]
    recip_bc = arena[:, 8448:8960]
    k2b = arena_bf[:, 17920:19072]
    q2b = arena_bf[:, 19072:19584]
    wkv_ring = arena_bf[:, 19584:23680]
    wq_ring = arena_bf[:, 23680:27776]
    wo_ring = arena_bf[:, 27776:31872]

    ps_a = nc.alloc_psum_tensor("ps_a", [128, 2048], F32)
    ps_b = nc.alloc_psum_tensor("ps_b", [128, 2048], F32)

    plan = {e: [] for e in ENGINES}

    class Ctr:
        __slots__ = ("name", "step", "n", "h")

        def __init__(self, name, step):
            self.name, self.step, self.n, self.h = name, step, 0, None

        def inc(self, k=1):
            self.n += k * self.step
            return (self, k * self.step)

    c_ldx = Ctr("ldx", 16)
    c_outs = [Ctr(f"ldo{i}", 16) for i in range(4)]
    c_xins = [Ctr(f"ldi{i}", 16) for i in range(4)]
    c_pe = Ctr("pe", 1)
    c_dv = Ctr("dv", 1)
    c_se = Ctr("se", 1)
    CTRS = [c_ldx, c_pe, c_dv, c_se] + c_outs + c_xins

    class Ring:

        def __init__(self, name, nslots):
            self.ctrs = [Ctr(f"{name}{i}", 16) for i in range(nslots)]
            self.free_at = [0] * nslots
            self.n = nslots
            CTRS.extend(self.ctrs)

        def load(self, eng, slot, fn, extra_waits=()):
            wts = list(extra_waits)
            if self.free_at[slot]:
                wts.append(W(c_pe, self.free_at[slot]))
            op(eng, fn, waits=wts, incs=[self.ctrs[slot].inc()])
            return W(self.ctrs[slot])

        def bump_all(self, val):
            for i in range(self.n):
                self.free_at[i] = max(self.free_at[i], val)

    r_wg = Ring("rwg", 4)
    r_wu = Ring("rwu", 4)
    r_wd = Ring("rwd", 4)
    r_wkv = Ring("rkv", 4)
    r_wq = Ring("rwq", 2)
    r_wo = Ring("rwo", 2)

    pending_edge = {}

    def op(eng, fn, waits=(), incs=()):
        pe_ = pending_edge.pop(eng, None)
        w = tuple(waits) + ((pe_,) if pe_ else ())
        plan[eng].append((w, fn, tuple(incs)))

    def W(ctr, val=None):
        return (ctr, ctr.n if val is None else val)

    def dma(dst, src):
        return lambda g, d=dst, s=src: g.dma_start(d, s)

    def edge(eng):
        c = {"DVE": c_dv, "SE": c_se, "PE": c_pe}[eng]
        waits, fn, incs = plan[eng][-1]
        for (ctr, amt) in incs:
            if ctr is c:
                pending_edge[eng] = (c, c.n)
                return
        assert not incs, "op already has a different-counter inc"
        plan[eng][-1] = (waits, fn, incs + (c.inc(),))
        pending_edge[eng] = (c, c.n)

    st_ = {
        "last_a": W(c_dv, 0), "last_b": W(c_dv, 0),
        "xres_wr": [W(c_dv, 0)] * 4,
        "rope_done": W(c_dv, 0),
        "cos_ready": 0,
        "out_done": [0, 0, 0, 0],
        "xn_ready": W(c_dv, 0),
        "tr_pe": [0, 0],
    }
    wg_serial = [0]
    wd_serial = [0]

    for dst, src in ((stripb[:, :], strip_h[:, :]), (mb1b[:, :], mb1_h[:, :]),
                     (mmetab[:, :], mmeta_h[:, :]), (idenb[:, :], iden_h[:, :]),
                     (propeb[:, :], prope_h[:, :]), (onecb[:, :], onec_h[:, :]),
                     (onerb[:, :], oner_h[:, :]), (bqraw[:, :], bq_h[:, :]),
                     (bkraw[:, :], bkv_h[:, :])):
        op("POOL", dma(dst, src), incs=[c_ldx.inc()])
    consts_ld = c_ldx.n
    op("POOL", dma(a_f[0:1, 0:2048], bo_h[:, :]), incs=[c_ldx.inc()])
    op("POOL", dma(tmp4[0:1, 0:512], bkvv_h[:, :]), incs=[c_ldx.inc()])
    bcast_ld = c_ldx.n

    op("PE", lambda t: t.transpose(ps_a[:, 0:16], bqraw[:, :], idenb[0:16, 0:16]),
       waits=[W(c_ldx, bcast_ld)], incs=[c_pe.inc()])
    op("PE", lambda t: t.transpose(ps_a[:, 512:520], bkraw[:, :], idenb[0:8, 0:8]),
       incs=[c_pe.inc()])
    pe_bias = c_pe.n
    op("SE", lambda s: s.activation(bq_s[:, :], ps_a[:, 0:16], AF.Identity, scale=SCALE),
       waits=[W(c_pe, pe_bias)])
    op("SE", lambda s: s.activation(bkvk_s[:, :], ps_a[:, 512:520], AF.Identity),
       incs=[c_se.inc()])
    se_bias = c_se.n
    for j in range(4):
        op("PE", lambda t, j=j: t.matmul(ps_b[:, j * 512:(j + 1) * 512], onerb[:, :],
                                         a_f[0:1, j * 512:(j + 1) * 512],
                                         start=True, stop=True),
           waits=([W(c_ldx, bcast_ld)] if j == 0 else []))
    op("PE", lambda t: t.matmul(ps_a[:, 0:512], onerb[:, :], tmp4[0:1, 0:512],
                                start=True, stop=True),
       waits=[W(c_se, se_bias)], incs=[c_pe.inc()])
    pe_bcast = c_pe.n
    op("DVE", lambda v: v.tensor_copy(bo_bc[:, :], ps_b[:, :]),
       waits=[W(c_pe, pe_bcast)])
    op("DVE", lambda v: v.tensor_copy(bkvv_bc[:, :], ps_a[:, 0:512]))
    op("DVE", lambda v: v.memset(stats[:, :], 0.0))
    op("DVE", lambda v: v.memset(k_ctx[:, :], 0.0))
    op("DVE", lambda v: v.memset(v_ctx[:, :], 0.0), incs=[c_dv.inc()])
    st_["last_a"] = W(c_dv)
    st_["last_b"] = W(c_dv)

    for sub in range(4):
        dst0 = x_res[:, sub * 2048:(sub + 1) * 2048]
        src0 = x_h[sub * 128:(sub + 1) * 128, :]
        wts0 = [W(c_ldx)] if sub == 0 else []
        op("POOL", lambda g, d=dst0, s=src0: g.dma_start(d, s),
           waits=wts0, incs=[c_xins[sub].inc()])
    op("POOL", lambda g: g.dma_start(cos_t[:, :], cos_h[:, 0:512]),
       incs=[c_ldx.inc()])
    op("POOL", lambda g: g.dma_start(sin_t[:, :], sin_h[:, 0:512]),
       incs=[c_ldx.inc()])

    cast_after = {}
    cast_sems = []

    def cast_dma(key, fn):
        c = Ctr(f"cs{len(cast_sems)}", 16)
        cast_sems.append(c)
        CTRS.append(c)
        op("POOL", fn, incs=[c.inc()])
        cast_after[key] = W(c, 16)

    def cast_wg(pc):
        cast_dma(("wg2", pc), lambda g, pc=pc: g.dma_start(
            wgb[pc * 2:(pc + 1) * 2], wg_h[pc * 2:(pc + 1) * 2]))
        cast_dma(("wu2", pc), lambda g, pc=pc: g.dma_start(
            wub[pc * 2:(pc + 1) * 2], wu_h[pc * 2:(pc + 1) * 2]))
        cast_after[("wg", pc)] = cast_after[("wg2", pc)]
        cast_after[("wu", pc)] = cast_after[("wu2", pc)]

    def cast_wdq(pc):
        cast_dma(("wdq", pc), lambda g, pc=pc: g.dma_start(
            wdb[pc * 2048:(pc + 1) * 2048, :], wd_h[pc * 2048:(pc + 1) * 2048, :]))

    cast_wg(0); cast_wg(1); cast_wg(2); cast_wg(3)
    cast_wdq(0); cast_wdq(1)
    cast_wg(4); cast_wg(5); cast_wg(6); cast_wg(7)
    cast_wdq(2); cast_wdq(3)
    cast_dma("wkv", lambda g: g.dma_start(wkvb[:, :], wkv_h[:, :]))
    cast_dma("wq", lambda g: g.dma_start(wqb[:, :, :, :], wq_h[:, :, :, :]))
    cast_dma("wo", lambda g: g.dma_start(wob[:, :], wo_h[:, :]))

    ln_round = [0]
    ln_evac = [0, 0]

    def ln_emit():
        ln_evac[0] = ln_evac[1] = 0
        ln_round[0] = 1
        tr_pe = st_["tr_pe"]
        for sub in range(4):
            xa = x_res[:, sub * 2048:(sub + 1) * 2048]
            wx = st_["xres_wr"][sub]
            op("DVE", lambda v, xa=xa, sub=sub: v.tensor_reduce(
                stats[:, sub:sub + 1], xa, AX.X, ALU.add),
               waits=[wx, W(c_dv)])
            op("SE", lambda s, xa=xa: s.activation(h_f[:, :], xa, AF.Square),
               waits=[wx, W(c_dv)], incs=[c_se.inc()])
            op("DVE", lambda v, sub=sub: v.tensor_reduce(
                stats[:, 4 + sub:5 + sub], h_f[:, :], AX.X, ALU.add),
               waits=[W(c_se)], incs=[c_dv.inc()])
            edge("DVE")
            op("DVE", lambda v, sub=sub: v.tensor_scalar_mul(
                stats[:, 8 + sub:9 + sub], stats[:, sub:sub + 1], -1.0 / 2048.0))
            op("DVE", lambda v, sub=sub: v.tensor_scalar_mul(
                stats[:, 12 + sub:13 + sub], stats[:, 4 + sub:5 + sub], 1.0 / 2048.0))
            edge("DVE")
            op("DVE", lambda v, sub=sub: v.tensor_mul(
                stats[:, 16 + sub:17 + sub], stats[:, 8 + sub:9 + sub],
                stats[:, 8 + sub:9 + sub]))
            edge("DVE")
            op("DVE", lambda v, sub=sub: v.tensor_scalar(
                stats[:, 20 + sub:21 + sub], stats[:, 12 + sub:13 + sub],
                stats[:, 16 + sub:17 + sub], EPS, ALU.subtract, ALU.add),
               incs=[c_dv.inc()])
            op("SE", lambda s, sub=sub: s.activation(
                stats[:, 24 + sub:25 + sub], stats[:, 20 + sub:21 + sub], AF.Ln),
               waits=[W(c_dv)])
            edge("SE")
            op("SE", lambda s, sub=sub: s.activation(
                stats[:, 28 + sub:29 + sub], stats[:, 24 + sub:25 + sub], AF.Exp,
                scale=-0.5),
               incs=[c_se.inc()])
            op("DVE", lambda v, sub=sub: v.tensor_mul(
                stats[:, 8 + sub:9 + sub], stats[:, 8 + sub:9 + sub],
                stats[:, 28 + sub:29 + sub]),
               waits=[W(c_se)], incs=[c_dv.inc()])
            nb_dv = c_dv.n
            nbuf = tmp4 if sub % 2 == 0 else a_f
            wts = [W(c_dv, nb_dv)]
            if tr_pe[sub % 2]:
                wts.append(W(c_pe, tr_pe[sub % 2]))
            op("SE", lambda s, xa=xa, nb=nbuf, sub=sub: s.activation(
                nb[:, :], xa, AF.Identity, bias=stats[:, 8 + sub:9 + sub],
                scale=stats[:, 28 + sub:29 + sub]),
               waits=wts, incs=[c_se.inc()])
            se_norm = c_se.n
            for rnd in range(4):
                gr = ln_round[0]
                ln_round[0] += 1
                ps = ps_b if gr % 2 == 0 else ps_a
                for j in range(4):
                    e = rnd * 4 + j
                    dst = ps[:, j * 512:j * 512 + 128]
                    srcb = nbuf[:, e * 128:(e + 1) * 128]
                    w = []
                    if j == 0:
                        w = [W(c_se, se_norm)]
                        if ln_evac[gr % 2]:
                            w.append(W(c_dv, ln_evac[gr % 2]))
                        else:
                            w.append(st_["last_b"] if gr % 2 == 0 else st_["last_a"])
                    op("PE", lambda t, d=dst, s=srcb: t.transpose(d, s, idenb[:, :]),
                       waits=w, incs=([c_pe.inc()] if j == 3 else []))
                src3 = ps[:, :].rearrange("p (b c) -> p b c", c=512)[:, :, 0:128]
                dst3 = xn_fm[:, :].rearrange("p (e c) -> p e c", c=512)[
                    :, rnd * 4:(rnd + 1) * 4, sub * 128:(sub + 1) * 128]
                op("DVE", lambda v, s3=src3, d3=dst3: v.tensor_copy(d3, s3),
                   waits=[W(c_pe)], incs=[c_dv.inc()])
                ln_evac[gr % 2] = c_dv.n
            tr_pe[sub % 2] = c_pe.n
        st_["last_b"] = W(c_dv, ln_evac[0])
        st_["last_a"] = W(c_dv, ln_evac[1])
        st_["xn_ready"] = W(c_dv)

    hs_ready = [0] * 8

    def ph1(half):
        for ql in range(8):
            q = half * 8 + ql
            base_srl = wg_serial[0]
            wff_at = {}
            for hh in range(4):
                srl = base_srl + hh
                slot = srl % 4
                dg = wg_ring[:, slot * 2048:(slot + 1) * 2048].rearrange(
                    "p (h c) -> p h c", c=512)
                du = wu_ring[:, slot * 2048:(slot + 1) * 2048].rearrange(
                    "p (h c) -> p h c", c=512)
                wg_w = r_wg.load("SP", slot,
                                 lambda g, d=dg, s=wgb[q, hh]: g.dma_start(d, s),
                                 extra_waits=[cast_after[("wg", q // 2)]])
                wu_w = r_wu.load("SE", slot,
                                 lambda g, d=du, s=wub[q, hh]: g.dma_start(d, s),
                                 extra_waits=[cast_after[("wu", q // 2)]])
                wff_at[hh] = (wg_w, wu_w)
            wg_serial[0] += 4
            for h in range(16):
                hh, h4 = h // 4, h % 4
                slot = (base_srl + hh) % 4
                for m in range(4):
                    off = slot * 2048 + h4 * 512 + m * 128
                    lg = wg_ring[:, off:off + 128]
                    rhs = xn_fm[:, h * 512:(h + 1) * 512]
                    og = ps_a[:, m * 512:(m + 1) * 512]
                    wts = []
                    if h == 0 and m == 0:
                        wts = [st_["last_a"], st_["xn_ready"]]
                    if m == 0 and h4 == 0:
                        wts.append(wff_at[hh][0])
                    incs = [c_pe.inc()] if (h == 15 and m == 3) else []
                    op("PE", lambda t, o=og, l=lg, r=rhs, st=(h == 0), sp=(h == 15):
                       t.matmul(o, l, r, start=st, stop=sp), waits=wts, incs=incs)
            pe_g_done = c_pe.n
            op("SE", lambda s: s.activation(h_f[:, :], ps_a[:, :], AF.Silu),
               waits=[W(c_pe, pe_g_done), W(c_dv)], incs=[c_se.inc()])
            st_["last_a"] = W(c_se)
            for h in range(16):
                hh, h4 = h // 4, h % 4
                slot = (base_srl + hh) % 4
                for m in range(4):
                    off = slot * 2048 + h4 * 512 + m * 128
                    lu = wu_ring[:, off:off + 128]
                    rhs = xn_fm[:, h * 512:(h + 1) * 512]
                    ou = ps_b[:, m * 512:(m + 1) * 512]
                    wts = []
                    if h == 0 and m == 0:
                        wts = [st_["last_b"], st_["xn_ready"]]
                    if m == 0 and h4 == 0:
                        wts.append(wff_at[hh][1])
                    incs = []
                    if h4 == 3 and m == 3:
                        incs = [c_pe.inc()]
                    op("PE", lambda t, o=ou, l=lu, r=rhs, st=(h == 0), sp=(h == 15):
                       t.matmul(o, l, r, start=st, stop=sp), waits=wts, incs=incs)
                    if h4 == 3 and m == 3:
                        r_wg.free_at[slot] = c_pe.n
                        r_wu.free_at[slot] = c_pe.n
            pe_done = c_pe.n
            op("DVE", lambda v: v.tensor_mul(h_f[:, :], h_f[:, :], ps_b[:, :]),
               waits=[W(c_pe, pe_done), W(c_se)], incs=[c_dv.inc()])
            st_["last_b"] = W(c_dv)
            edge("DVE")
            op("DVE", lambda v: v.tensor_single_scalar(
                a_f[:, :].bitcast(mybir.dt.int32), h_f[:, :].bitcast(mybir.dt.int32),
                0x7FFFFFFF, ALU.bitwise_and))
            sl = [a_f[:, i * 512:(i + 1) * 512] for i in range(4)]
            tm = [tmp4[:, i * 512:(i + 1) * 512] for i in range(4)]
            mb = [mskf[:, i * 512:(i + 1) * 512] for i in range(3)]
            edge("DVE")
            op("DVE", lambda v: v.tensor_max(tm[0], sl[0], sl[1]))
            op("DVE", lambda v: v.tensor_max(tm[1], sl[2], sl[3]))
            op("DVE", lambda v: v.tensor_tensor(tm[2], sl[0], sl[1], ALU.min))
            op("DVE", lambda v: v.tensor_tensor(tm[3], sl[2], sl[3], ALU.min))
            edge("DVE")
            op("DVE", lambda v: v.tensor_tensor(tsel[:, :], tm[0], tm[1], ALU.min))
            op("DVE", lambda v: v.tensor_max(mb[2], tm[2], tm[3]))
            edge("DVE")
            op("DVE", lambda v: v.tensor_max(tsel[:, :], tsel[:, :], mb[2]))
            edge("DVE")
            bufs = [0, 1, 2, 0]

            def cmp_m(m):
                op("DVE", lambda v, m=m: v.tensor_tensor(
                    mb[bufs[m]], sl[m], tsel[:, :], ALU.is_ge))

            def mul_m(m, inc=False):
                c_loc = ql * 4 + m
                dsth = hs[:, c_loc * 512:(c_loc + 1) * 512]
                op("DVE", lambda v, m=m, d=dsth: v.tensor_mul(
                    d, h_f[:, m * 512:(m + 1) * 512], mb[bufs[m]]),
                   incs=([c_dv.inc()] if inc else []))

            cmp_m(0); cmp_m(1); cmp_m(2)
            edge("DVE")
            mul_m(0); mul_m(1)
            edge("DVE")
            cmp_m(3); mul_m(2)
            edge("DVE")
            mul_m(3, inc=True)
            hs_ready[ql] = c_dv.n

    def ph2(half):
        for pair in range(2):
            subs = (0, 1) if pair == 0 else (2, 3)
            yps = {subs[0]: ps_a, subs[1]: ps_b}
            for i in range(32):
                rowbase = half * 4096 + i * 128
                slot = wd_serial[0] % 4
                wd_serial[0] += 1
                dst = wd_ring[:, slot * 2048:(slot + 1) * 2048]
                wd_at = r_wd.load(
                    "SE" if slot % 2 == 0 else "SP", slot,
                    lambda s, d=dst, sc=wdb[rowbase:rowbase + 128, :]: s.dma_start(d, sc),
                    extra_waits=[cast_after[("wdq", half * 2 + i // 16)]])
                for sub in subs:
                    lhs = hs[:, i * 512 + sub * 128:i * 512 + sub * 128 + 128]
                    for ocol in range(4):
                        o = yps[sub][:, ocol * 512:(ocol + 1) * 512]
                        r = wd_ring[:, slot * 2048 + ocol * 512:slot * 2048 + (ocol + 1) * 512]
                        wts2 = []
                        if i == 0 and ocol == 0:
                            wts2.append(st_["last_a"] if sub == subs[0] else st_["last_b"])
                        if ocol == 0 and sub == subs[0]:
                            wts2.append(W(c_dv, hs_ready[i // 4]))
                            wts2.append(wd_at)
                        incs = []
                        if sub == subs[1] and ocol == 3:
                            incs = [c_pe.inc()]
                        op("PE", lambda t, o=o, l=lhs, r=r, st=(i == 0), sp=(i == 31):
                           t.matmul(o, l, r, start=st, stop=sp), waits=wts2, incs=incs)
                        if sub == subs[1] and ocol == 3:
                            r_wd.free_at[slot] = c_pe.n
            pe_y = c_pe.n
            for sub in subs:
                xa = x_res[:, sub * 2048:(sub + 1) * 2048]
                op("DVE", lambda v, xa=xa, p=yps[sub]: v.scalar_tensor_tensor(
                    xa, p[:, :], 0.5, xa, ALU.mult, ALU.add),
                   waits=[W(c_pe, pe_y)], incs=[c_dv.inc()])
                st_["xres_wr"][sub] = W(c_dv)
                if sub == subs[0]:
                    st_["last_a"] = W(c_dv)
                else:
                    st_["last_b"] = W(c_dv)

    def attn_emit(t_i):
        for h in range(16):
            slot = h % 4
            dst = wkv_ring[:, slot * 1024:(slot + 1) * 1024]
            wat_w = r_wkv.load(
                "SP", slot,
                lambda g, d=dst, s=wkvb[h * 128:(h + 1) * 128, :]: g.dma_start(d, s),
                extra_waits=[cast_after["wkv"]])
            for g_ in range(4):
                lk = wkv_ring[:, slot * 1024 + g_ * 128:slot * 1024 + (g_ + 1) * 128]
                rhs = xn_fm[:, h * 512:(h + 1) * 512]
                o = ps_a[:, g_ * 512:(g_ + 1) * 512]
                wts2 = []
                if g_ == 0:
                    wts2.append(wat_w)
                    if h == 0:
                        wts2 += [st_["last_a"], st_["last_b"]]
                op("PE", lambda t, o=o, l=lk, r=rhs, st=(h == 0), sp=(h == 15):
                   t.matmul(o, l, r, start=st, stop=sp), waits=wts2)
            for sub in range(4):
                lx = xn_fm[:, h * 512 + sub * 128:h * 512 + sub * 128 + 128]
                rv = wkv_ring[:, slot * 1024 + 512:slot * 1024 + 1024]
                o = ps_b[:, sub * 512:(sub + 1) * 512]
                op("PE", lambda t, o=o, l=lx, r=rv, st=(h == 0), sp=(h == 15):
                   t.matmul(o, l, r, start=st, stop=sp),
                   incs=([c_pe.inc()] if sub == 3 else []))
                if sub == 3:
                    r_wkv.free_at[slot] = c_pe.n
        pe_kv = c_pe.n
        for sub in range(4):
            dst = v_ctx[:, (5 + sub) * 512:(6 + sub) * 512]
            o = ps_b[:, sub * 512:(sub + 1) * 512]
            op("DVE", lambda v, d=dst, o=o: v.tensor_add(d, o, bkvv_bc[:, :]),
               waits=([W(c_pe, pe_kv)] if sub == 0 else []),
               incs=([c_dv.inc()] if sub == 3 else []))
        st_["last_b"] = W(c_dv)
        prev_rot_read = W(c_dv, 0)
        for g_ in range(4):
            kps = ps_a[:, g_ * 512:(g_ + 1) * 512]
            wts = [W(c_pe, pe_kv)]
            if g_ > 0:
                wts.append(prev_rot_read)
                wts.append(W(c_pe))
            op("SE", lambda s, o=kps, g_=g_: s.activation(
                qb_f[:, 0:512], o, AF.Identity, bias=bkvk_s[:, g_:g_ + 1]),
               waits=wts, incs=[c_se.inc()])
            se_kb = c_se.n
            op("PE", lambda t: t.matmul(ps_b[:, 0:512], propeb[:, :], qb_f[:, 0:512],
                                        start=True, stop=True),
               waits=[W(c_se, se_kb), st_["last_b"], W(c_ldx, st_["cos_ready"])],
               incs=[c_pe.inc()])
            pe_rot = c_pe.n
            op("DVE", lambda v: v.tensor_mul(rt1[:, :], qb_f[:, 0:512], cos_t[:, :]),
               waits=[W(c_pe, pe_rot)])
            op("DVE", lambda v: v.tensor_mul(rt2[:, :], ps_b[:, 0:512], sin_t[:, :]))
            edge("DVE")
            kdst = k_ctx[:, g_ * 1152 + 640:g_ * 1152 + 1152]
            op("DVE", lambda v, d=kdst: v.tensor_add(d, rt1[:, :], rt2[:, :]),
               incs=[c_dv.inc()])
            prev_rot_read = W(c_dv)
            st_["last_b"] = W(c_dv)
        st_["last_a"] = W(c_dv)
        dv_kctx = c_dv.n

        qbuf_dv = [0, 0]
        qbank_se = [0, 0, 0, 0]
        for e in range(16):
            slot = e % 2
            dq = wq_ring[:, slot * 2048:(slot + 1) * 2048].rearrange(
                "p (h c) -> p h c", c=128)
            wat_w = r_wq.load("SP", slot,
                              lambda g, d=dq, s=wqb[e]: g.dma_start(d, s),
                              extra_waits=[cast_after["wq"]])
            bank = e % 4
            o = ps_a[:, bank * 512:(bank + 1) * 512]
            for h in range(16):
                l = wq_ring[:, slot * 2048 + h * 128:slot * 2048 + (h + 1) * 128]
                rhs = xn_fm[:, h * 512:(h + 1) * 512]
                wts2 = []
                if h == 0:
                    wts2.append(wat_w)
                    if e == 0:
                        wts2.append(st_["last_a"])
                    if qbank_se[bank]:
                        wts2.append(W(c_se, qbank_se[bank]))
                op("PE", lambda t, o=o, l=l, r=rhs, st=(h == 0), sp=(h == 15):
                   t.matmul(o, l, r, start=st, stop=sp), waits=wts2,
                   incs=([c_pe.inc()] if h == 15 else []))
                if h == 15:
                    r_wq.free_at[slot] = c_pe.n
            pe_q = c_pe.n
            qb = qb_f[:, (e % 2) * 512:(e % 2) * 512 + 512]
            wts = [W(c_pe, pe_q)]
            if qbuf_dv[e % 2]:
                wts.append(W(c_dv, qbuf_dv[e % 2]))
            op("SE", lambda s, o=o, qb=qb, e=e: s.activation(
                qb, o, AF.Identity, bias=bq_s[:, e:e + 1], scale=SCALE),
               waits=wts, incs=[c_se.inc()])
            qbank_se[bank] = c_se.n
            op("PE", lambda t, qb=qb: t.matmul(ps_b[:, 0:512], propeb[:, :], qb,
                                               start=True, stop=True),
               waits=[W(c_se), W(c_dv)], incs=[c_pe.inc()])
            pe_rot = c_pe.n
            op("DVE", lambda v, qb=qb: v.tensor_mul(rt1[:, :], qb, cos_t[:, :]),
               waits=[W(c_pe, pe_rot)])
            op("DVE", lambda v: v.tensor_mul(rt2[:, :], ps_b[:, 0:512], sin_t[:, :]))
            edge("DVE")
            qdst = q_fm[:, e * 512:(e + 1) * 512]
            op("DVE", lambda v, d=qdst: v.tensor_add(d, rt1[:, :], rt2[:, :]),
               incs=[c_dv.inc()])
            qbuf_dv[e % 2] = c_dv.n
        dv_q = c_dv.n
        se_qb_final = c_se.n
        st_["rope_done"] = W(c_dv, dv_q)

        if t_i == 0:
            blocks = list(range(5, 9))
        elif t_i == 1:
            blocks = list(range(1, 9))
        else:
            blocks = list(range(0, 9))

        prev_k2_pe = 0
        for g2 in range(4):
            wts = [W(c_dv, dv_kctx)]
            if prev_k2_pe:
                wts.append(W(c_pe, prev_k2_pe))
            op("SE", lambda s, g2=g2: s.activation(
                k2b[:, :], k_ctx[:, g2 * 1152:(g2 + 1) * 1152], AF.Square),
               waits=wts, incs=[c_se.inc()])
            se_k2 = c_se.n
            spans = [(0, 512, ps_b[0:1, 1024:1536]),
                     (512, 1024, ps_b[0:1, 1536:2048]),
                     (1024, 1152, ps_b[0:1, 512:640])]
            for si, (c0, c1, o) in enumerate(spans):
                wts2 = [W(c_se, se_k2)]
                if si == 0:
                    wts2.append(W(c_dv))
                    if g2 == 0:
                        wts2.append(st_["last_b"])
                op("PE", lambda t, o=o, c0=c0, c1=c1: t.matmul(
                    o, onecb[:, :], k2b[:, c0:c1], start=True, stop=True),
                   waits=wts2, incs=([c_pe.inc()] if si == 2 else []))
            prev_k2_pe = c_pe.n
            op("DVE", lambda v: v.tensor_reduce(srow[0:1, 4:5], ps_b[0:1, 1024:1536],
                                                AX.X, ALU.max),
               waits=[W(c_pe, prev_k2_pe)])
            op("DVE", lambda v: v.tensor_reduce(srow[0:1, 5:6], ps_b[0:1, 1536:2048],
                                                AX.X, ALU.max))
            op("DVE", lambda v: v.tensor_reduce(srow[0:1, 6:7], ps_b[0:1, 512:640],
                                                AX.X, ALU.max))
            edge("DVE")
            op("DVE", lambda v: v.tensor_max(srow[0:1, 4:5], srow[0:1, 4:5],
                                             srow[0:1, 5:6]))
            edge("DVE")
            op("DVE", lambda v, g2=g2: v.tensor_max(
                srow[0:1, g2:g2 + 1], srow[0:1, 4:5], srow[0:1, 6:7]),
               incs=[c_dv.inc()])
        c2_dv = c_dv.n
        st_["last_b"] = W(c_dv)

        nm_dv = {}
        sqrt_se = {}
        nmcopy_dv = {}
        lbank = {}

        def stats_emit(hd):
            g2 = hd // 4
            pp = hd % 2
            qsl2 = q_fm[:, hd * 512:(hd + 1) * 512]
            op("SE", lambda s, q=qsl2: s.activation(q2b[:, :], q, AF.Square),
               waits=[W(c_dv, dv_q)], incs=[c_se.inc()])
            se_q2 = c_se.n
            lb = ps_b[0:1, 1024:1536] if hd % 2 == 0 else ps_b[0:1, 0:512]
            lbank[hd] = lb
            wts = [W(c_se, se_q2)]
            if hd >= 2 and (hd - 2) in sqrt_se:
                wts.append(W(c_dv, recip_dv[hd - 2]))
            elif hd >= 1:
                wts.append(W(c_se, sqrt_se[hd - 1]))
            else:
                wts.append(W(c_dv, c2_dv))
            op("PE", lambda t, lb=lb: t.matmul(lb, onecb[:, :], q2b[:, :],
                                               start=True, stop=True),
               waits=wts, incs=[c_pe.inc()])
            pe_q2 = c_pe.n
            ar = arow[0:1, pp * 512:pp * 512 + 512]
            op("SE", lambda s, g2=g2, ar=ar, lb=lb: s.activation(
                ar, lb, AF.Sqrt, scale=srow[0:1, g2:g2 + 1]),
               waits=[W(c_pe, pe_q2), W(c_dv, c2_dv)], incs=[c_se.inc()])
            sqrt_se[hd] = c_se.n
            nr = nmrow[0:1, pp:pp + 1]
            op("DVE", lambda v, ar=ar, nr=nr: v.tensor_reduce(
                nr, ar, AX.X, ALU.max, negate=True),
               waits=[W(c_se)], incs=[c_dv.inc()])
            dv_nr = c_dv.n
            wts = [W(c_dv, dv_nr)]
            if (hd - 1) in nmcopy_dv:
                wts.append(W(c_dv, nmcopy_dv[hd - 1]))
            op("PE", lambda t, nr=nr: t.matmul(ps_b[:, 1536:1537], onerb[:, :], nr,
                                               start=True, stop=True),
               waits=wts, incs=[c_pe.inc()])
            pe_nm = c_pe.n
            op("DVE", lambda v, pp=pp: v.tensor_copy(nm128[:, pp:pp + 1],
                                                     ps_b[:, 1536:1537]),
               waits=[W(c_pe, pe_nm)], incs=[c_dv.inc()])
            nm_dv[hd] = c_dv.n
            nmcopy_dv[hd] = c_dv.n

        stats_emit(0)

        nb = len(blocks)
        triples = [blocks[i:i + 3] for i in range(0, nb, 3)]
        ntr = len(triples)
        exp3_done = {}
        mask_dv = {}
        recip_dv = {}
        ymul_dv = {}

        def emit_triple(hd, ti):
            g2 = hd // 4
            pp = hd % 2
            qsl2 = q_fm[:, hd * 512:(hd + 1) * 512]
            trp = triples[ti]
            for j, b in enumerate(trp):
                o = ps_a[:, j * 512:(j + 1) * 512]
                lk = k_ctx[:, g2 * 1152 + b * 128:g2 * 1152 + (b + 1) * 128]
                wts2 = []
                if j == 0:
                    if ti == 0 and hd == 0:
                        wts2 = [W(c_se, se_qb_final), W(c_dv, dv_q), st_["last_a"]]
                    else:
                        prev = exp3_done.get((hd, ti - 1)) or                             exp3_done.get((hd - 1, ntr - 1))
                        if prev:
                            wts2 = [W(c_se, prev)]
                op("PE", lambda t, o=o, l=lk, q=qsl2: t.matmul(
                    o, l, q, start=True, stop=True),
                   waits=wts2, incs=([c_pe.inc()] if j == len(trp) - 1 else []))
            pe_s = c_pe.n
            b0 = trp[0]
            w = len(trp) * 512
            pdst = p_sb[:, b0 * 512:b0 * 512 + w]
            op("SE", lambda s, d=pdst, w=w, pp=pp: s.activation(
                d, ps_a[:, 0:w], AF.Exp, bias=nm128[:, pp:pp + 1]),
               waits=[W(c_pe, pe_s), W(c_dv, nm_dv[hd])], incs=[c_se.inc()])
            exp3_done[(hd, ti)] = c_se.n
            for b in trp:
                pd = p_sb[:, b * 512:(b + 1) * 512]
                if b == 0:
                    msl = mmetab[:, :]
                elif t_i == 1 and b == 1:
                    msl = mb1b[:, :]
                else:
                    msl = stripb[:, 1024 - 128 * b:1536 - 128 * b]
                op("DVE", lambda v, d=pd, m=msl: v.tensor_mul(d, d, m),
                   waits=[W(c_se)], incs=[c_dv.inc()])
                mask_dv[(hd, b)] = c_dv.n

        def emit_lav(hd, ti):
            g2 = hd // 4
            base = sum(len(triples[t]) for t in range(ti))
            for k, b in enumerate(triples[ti]):
                bi = base + k
                psl = p_sb[:, b * 512:(b + 1) * 512]
                first, last = (bi == 0), (bi == nb - 1)
                wts2 = [W(c_dv, mask_dv[(hd, b)])]
                if first:
                    wts2.append(W(c_se, sqrt_se[hd]))
                op("PE", lambda t, p=psl, st=first, sp=last, lb=lbank[hd]: t.matmul(
                    lb, onecb[:, :], p, start=st, stop=sp),
                   waits=wts2)
                lv = v_ctx[:, b * 512 + g2 * 128:b * 512 + (g2 + 1) * 128]
                wts3 = []
                if first and hd > 0:
                    wts3.append(W(c_dv, ymul_dv[hd - 1]))
                op("PE", lambda t, l=lv, p=psl, st=first, sp=last: t.matmul(
                    ps_a[:, 1536:2048], l, p, start=st, stop=sp),
                   waits=wts3, incs=([c_pe.inc()] if last else []))

        def finalize(hd):
            pe_lav = c_pe.n
            op("DVE", lambda v, lb=lbank[hd]: v.reciprocal(linv[:, :], lb),
               waits=[W(c_pe, pe_lav)], incs=[c_dv.inc()])
            recip_dv[hd] = c_dv.n
            op("PE", lambda t: t.matmul(ps_b[:, 512:1024], onerb[:, :], linv[:, :],
                                        start=True, stop=True),
               waits=[W(c_dv)], incs=[c_pe.inc()])
            pe_rb = c_pe.n
            op("DVE", lambda v: v.tensor_copy(recip_bc[:, :], ps_b[:, 512:1024]),
               waits=[W(c_pe, pe_rb)])
            edge("DVE")
            ydst = q_fm[:, hd * 512:(hd + 1) * 512]
            op("DVE", lambda v, d=ydst: v.tensor_mul(d, ps_a[:, 1536:2048],
                                                     recip_bc[:, :]),
               incs=[c_dv.inc()])
            ymul_dv[hd] = c_dv.n
            st_["last_a"] = W(c_dv)

        for hd_i in range(16):
            for ti in range(ntr):
                emit_triple(hd_i, ti)
                if ti == ntr - 1 and hd_i + 1 < 16:
                    stats_emit(hd_i + 1)
                if ti >= 1:
                    emit_lav(hd_i, ti - 1)
                elif hd_i > 0:
                    emit_lav(hd_i - 1, ntr - 1)
                    finalize(hd_i - 1)
        emit_lav(15, ntr - 1)
        finalize(15)
        dv_y = c_dv.n

        pe_done = c_pe.n
        for g_ in range(4):
            src = k_ctx[:, g_ * 1152 + 640:g_ * 1152 + 1152]
            dst = k_ctx[:, g_ * 1152 + 128:g_ * 1152 + 640]
            op("DVE", lambda v, d=dst, s=src: v.tensor_copy(d, s),
               waits=([W(c_pe, pe_done)] if g_ == 0 else []))
        for b in range(4):
            src = v_ctx[:, (5 + b) * 512:(6 + b) * 512]
            dst = v_ctx[:, (1 + b) * 512:(2 + b) * 512]
            op("DVE", lambda v, d=dst, s=src: v.tensor_copy(d, s))
        if t_i == 0:
            for g_ in range(4):
                src = k_ctx[:, g_ * 1152 + 640:g_ * 1152 + 656]
                dst = k_ctx[:, g_ * 1152:g_ * 1152 + 16]
                op("DVE", lambda v, d=dst, s=src: v.tensor_copy(d, s))
            op("DVE", lambda v: v.tensor_copy(v_ctx[0:16, 0:512],
                                              v_ctx[0:16, 5 * 512:6 * 512]))
        op("DVE", lambda v: v.tensor_copy(stats[:, 32:33], stats[:, 33:34]),
           incs=[c_dv.inc()])

        for pairi in range(2):
            subs = (0, 1) if pairi == 0 else (2, 3)
            yps = {subs[0]: ps_a, subs[1]: ps_b}
            for e in range(16):
                slot = e % 2
                dst = wo_ring[:, slot * 2048:(slot + 1) * 2048]
                wat_w = r_wo.load(
                    "SP", slot,
                    lambda g, d=dst, s=wob[e * 128:(e + 1) * 128, :]: g.dma_start(d, s),
                    extra_waits=[cast_after["wo"]])
                for sub in subs:
                    lhs = q_fm[:, e * 512 + sub * 128:e * 512 + sub * 128 + 128]
                    for ocol in range(4):
                        o = yps[sub][:, ocol * 512:(ocol + 1) * 512]
                        r = wo_ring[:, slot * 2048 + ocol * 512:slot * 2048 + (ocol + 1) * 512]
                        wts2 = []
                        if e == 0 and ocol == 0:
                            wts2.append(st_["last_a"] if sub == subs[0] else st_["last_b"])
                            wts2.append(W(c_dv, dv_y))
                        if ocol == 0 and sub == subs[0]:
                            wts2.append(wat_w)
                        incs = []
                        if sub == subs[1] and ocol == 3:
                            incs = [c_pe.inc()]
                        op("PE", lambda t, o=o, l=lhs, r=r, st=(e == 0), sp=(e == 15):
                           t.matmul(o, l, r, start=st, stop=sp), waits=wts2, incs=incs)
                        if sub == subs[1] and ocol == 3:
                            r_wo.free_at[slot] = c_pe.n
            pe_o = c_pe.n
            for sub in subs:
                xa = x_res[:, sub * 2048:(sub + 1) * 2048]
                p = yps[sub]
                op("DVE", lambda v, xa=xa, p=p: v.tensor_add(xa, xa, p[:, :]),
                   waits=[W(c_pe, pe_o)])
                edge("DVE")
                op("DVE", lambda v, xa=xa: v.tensor_add(xa, xa, bo_bc[:, :]),
                   incs=[c_dv.inc()])
                st_["xres_wr"][sub] = W(c_dv)
                if sub == subs[0]:
                    st_["last_a"] = W(c_dv)
                else:
                    st_["last_b"] = W(c_dv)


    for t_i in range(n_tiles):
        if t_i > 0:
            for sub in range(4):
                dst = x_res[:, sub * 2048:(sub + 1) * 2048]
                src = x_h[t_i * 512 + sub * 128:t_i * 512 + (sub + 1) * 128, :]
                wts = [W(c_outs[sub], st_["out_done"][sub])]
                op("POOL", lambda g, d=dst, s=src: g.dma_start(d, s),
                   waits=wts, incs=[c_xins[sub].inc()])
            wts = [st_["rope_done"]]
            op("POOL", lambda g, t=t_i: g.dma_start(cos_t[:, :],
                                                    cos_h[:, t * 512:(t + 1) * 512]),
               waits=wts, incs=[c_ldx.inc()])
            op("POOL", lambda g, t=t_i: g.dma_start(sin_t[:, :],
                                                    sin_h[:, t * 512:(t + 1) * 512]),
               incs=[c_ldx.inc()])
        st_["cos_ready"] = c_ldx.n
        for sub in range(4):
            st_["xres_wr"][sub] = W(c_xins[sub], c_xins[sub].n)

        ln_emit()
        ph1(0); ph2(0); ph1(1); ph2(1)
        r_wkv.bump_all(c_pe.n); r_wq.bump_all(c_pe.n); r_wo.bump_all(c_pe.n)
        ln_emit()
        attn_emit(t_i)
        r_wg.bump_all(c_pe.n); r_wu.bump_all(c_pe.n)
        ln_emit()
        ph1(0); ph2(0); ph1(1); ph2(1)
        r_wkv.bump_all(c_pe.n); r_wq.bump_all(c_pe.n); r_wo.bump_all(c_pe.n)
        for sub in range(4):
            src = x_res[:, sub * 2048:(sub + 1) * 2048]
            dst = out_h[t_i * 512 + sub * 128:t_i * 512 + (sub + 1) * 128, :]
            op("POOL", lambda g, d=dst, s=src: g.dma_start(d, s),
               waits=[st_["xres_wr"][sub]], incs=[c_outs[sub].inc()])
            st_["out_done"][sub] = c_outs[sub].n

    final_outs = [c.n for c in c_outs]

    with contextlib.ExitStack() as stk:
        for c in CTRS:
            if c.name != "wat_dummy":
                c.h = stk.enter_context(nc.semaphore(c.name))
        blk = stk.enter_context(nc.Block())

        def replay(name, eng):
            hw = {}
            for waits, fn, incs in plan[name]:
                for (ctr, val) in waits:
                    if val <= 0 or hw.get(ctr.name, 0) >= val:
                        continue
                    eng.wait_ge(ctr.h, val)
                    hw[ctr.name] = val
                instr = fn(eng)
                for (ctr, amt) in incs:
                    instr = instr.then_inc(ctr.h, amt)

        @blk.tensor
        def _(t):
            replay("PE", t)

        @blk.vector
        def _(v):
            replay("DVE", v)

        @blk.scalar
        def _(s):
            replay("SE", s)

        @blk.sync
        def _(s):
            replay("SP", s)

        @blk.gpsimd
        def _(g):
            replay("POOL", g)
            for c, v in zip(c_outs, final_outs):
                if v:
                    g.wait_ge(c.h, v)

    return nc


def _perm():
    c = np.arange(II)
    return 512 * (c // 512) + 4 * (c % 128) + (c % 512) // 128


def _host_constants():
    import ml_dtypes
    inv = 1.0 / 10000.0 ** (np.arange(0, HD, 2, dtype=np.float64) / HD)
    ang = np.arange(S, dtype=np.float64)[None, :] * inv[:, None]
    cosfm = np.ascontiguousarray(np.repeat(np.cos(ang), 2, axis=0).astype(np.float32))
    sinfm = np.ascontiguousarray(np.repeat(np.sin(ang), 2, axis=0).astype(np.float32))
    k = np.arange(128)[:, None]
    u = np.arange(1408)[None, :]
    band = ((u - 384 - k >= 0) & (u - 384 - k <= 511)).astype(np.float32)
    strip = band.astype(ml_dtypes.bfloat16)
    b1 = band[:, 896:1408].copy()
    b1[k[:, 0] < 16, :] = 1.0
    maskb1 = b1.astype(ml_dtypes.bfloat16)
    maskmeta = np.zeros((128, 512), np.float32)
    maskmeta[:16, :] = 1.0
    maskmeta = maskmeta.astype(ml_dtypes.bfloat16)
    iden = np.eye(128, dtype=np.float32)
    pr = np.zeros((128, 128), np.float32)
    for i in range(64):
        pr[2 * i + 1, 2 * i] = -1.0
        pr[2 * i, 2 * i + 1] = 1.0
    onec = np.ones((128, 1), ml_dtypes.bfloat16)
    oner = np.ones((1, 128), np.float32)
    return dict(cosfm=cosfm, sinfm=sinfm, strip=strip, maskb1=maskb1,
                maskmeta=maskmeta, idenf=iden, prope=pr, onec_bf=onec,
                oner_f=oner)


def _tile_weights(Wq, w_gate, w_up, w_down):
    perm = _perm()
    wg_p = w_gate[:, perm]
    wu_p = w_up[:, perm]
    wd_p = np.ascontiguousarray(w_down[perm, :])
    a = wg_p.reshape(4, 4, 128, 16, 512)
    wg_t = np.ascontiguousarray(np.transpose(a, (3, 0, 2, 1, 4)))
    a = wu_p.reshape(4, 4, 128, 16, 512)
    wu_t = np.ascontiguousarray(np.transpose(a, (3, 0, 2, 1, 4)))
    a = Wq.reshape(16, 128, 16, 128)
    wq_t = np.ascontiguousarray(np.transpose(a, (2, 1, 0, 3)))
    return wg_t, wu_t, wd_p, wq_t


def make_inputs_common(Wq, bq, Wkv, bkv, Wo, bo, wg, wu, wd):
    wg_t, wu_t, wd_p, wq_t = _tile_weights(Wq, wg, wu, wd)
    consts = _host_constants()
    return {
        "wg_t": wg_t, "wu_t": wu_t, "wd_t": wd_p, "wq_t": wq_t,
        "wkv_t": np.ascontiguousarray(Wkv), "wo_t": np.ascontiguousarray(Wo),
        "bq": np.ascontiguousarray(bq.reshape(16, 128)),
        "bkv": np.ascontiguousarray(bkv.reshape(8, 128)),
        "bkvv": np.ascontiguousarray(bkv[512:].reshape(1, 512)),
        "bo_r": np.ascontiguousarray(bo.reshape(1, E)),
        **consts,
    }


def kernel(L0_attn_x, L0_attn_Wq, L0_attn_bq, L0_attn_Wkv, L0_attn_bkv,
           L0_attn_Wo, L0_attn_bo, L0_ffn_w_gate, L0_ffn_w_up, L0_ffn_w_down):
    x = np.asarray(L0_attn_x, dtype=np.float32)
    common = make_inputs_common(
        np.asarray(L0_attn_Wq, np.float32), np.asarray(L0_attn_bq, np.float32),
        np.asarray(L0_attn_Wkv, np.float32), np.asarray(L0_attn_bkv, np.float32),
        np.asarray(L0_attn_Wo, np.float32), np.asarray(L0_attn_bo, np.float32),
        np.asarray(L0_ffn_w_gate, np.float32), np.asarray(L0_ffn_w_up, np.float32),
        np.asarray(L0_ffn_w_down, np.float32))
    nc = build_nc(n_tiles=8)
    in_maps = [dict(common, x=np.ascontiguousarray(x[b])) for b in range(8)]
    res = run_bass_kernel_spmd(nc, in_maps, core_ids=list(range(8)))
    return np.stack([res.results[b]["out"] for b in range(8)], axis=0)
